# version 02
# baseline (speedup 1.0000x reference)
import math
import numpy as np
import ml_dtypes
from contextlib import ExitStack
import concourse.bass as bass
import concourse.mybir as mybir
from concourse.bass_utils import run_bass_kernel_spmd

F32 = mybir.dt.float32
BF16 = mybir.dt.bfloat16
AF = mybir.ActivationFunctionType
ALU = mybir.AluOpType
AXL = mybir.AxisListType

NEG = -1.0e30

CFG_FULL = dict(E=1024, H=16, HID=4096, NE=8, T=2048, NSEQ=2, R=8)
CFG_SIM = dict(E=128, H=2, HID=512, NE=8, T=256, NSEQ=2, R=8)

_CHAIN_OPS = {
    "activation", "copy", "add", "mul", "sqrt", "square", "sign", "memset",
    "memzero", "tensor_copy", "tensor_tensor", "tensor_add", "tensor_sub",
    "tensor_mul", "tensor_max", "tensor_scalar", "tensor_single_scalar",
    "tensor_scalar_add", "tensor_scalar_sub", "tensor_scalar_mul",
    "tensor_scalar_max", "tensor_scalar_min", "tensor_relu", "tensor_reduce",
    "reduce_max", "reduce_sum", "reciprocal", "select", "copy_predicated",
    "tensor_tensor_reduce", "scalar_tensor_tensor", "tensor_tensor_scan",
    "iota", "bn_stats", "bn_aggr", "max", "max_index", "stream_shuffle",
    "affine_select",
}


class Ctr:

    def __init__(self, h):
        self.h = h
        self.n = 0

    def inc(self, instr, k=1):
        instr.then_inc(self.h, k)
        self.n += k
        return self.n

    def wait(self, eng, n=None):
        v = self.n if n is None else n
        if v > 0:
            eng.wait_ge(self.h, v)


class Chain:

    def __init__(self, eng, ctr):
        self._e = eng
        self._c = ctr

    def __getattr__(self, name):
        e = self._e
        a = getattr(e, name)
        if name not in _CHAIN_OPS:
            return a
        c = self._c

        def wrapped(*args, **kw):
            if c.n > 0:
                e.wait_ge(c.h, c.n)
            instr = a(*args, **kw)
            c.inc(instr)
            return instr

        return wrapped


def build(cfg, debug=False):
    E, H, HID, NE, T, NSEQ, R = (cfg["E"], cfg["H"], cfg["HID"], cfg["NE"],
                                 cfg["T"], cfg["NSEQ"], cfg["R"])
    DH = 64
    assert E == H * DH
    EC = E // 128
    HC = HID // 128
    NT = T * NSEQ
    NTC = NT // 128
    TBL = min(512, NT)
    NTB = NT // TBL
    TCB = TBL // 128
    QTN = T // 128
    PB = min(512, T)
    NPB = T // PB
    assert NPB <= 4
    EW = min(512, E)
    EH = E // EW
    SCL = 0.125
    LORA_S = 2.0
    SW = min(2 * T, 2048)
    start = 2 ** (-2 ** (-(math.log2(H) - 3)))
    slopes = [start * start ** i for i in range(H)]
    W2G = 4 if HC % 4 == 0 else 1

    nc = bass.Bass("TRN2", target_bir_lowering=False)

    x_d = nc.dram_tensor("x", [NT, E], F32, kind="ExternalInput")
    out_d = nc.dram_tensor("out", [NT, E], F32, kind="ExternalOutput")
    wqT_d = nc.dram_tensor("wqT", [E, E], F32, kind="ExternalInput")
    wkT_d = nc.dram_tensor("wkT", [E, E], F32, kind="ExternalInput")
    wvT_d = nc.dram_tensor("wvT", [E, E], F32, kind="ExternalInput")
    wpT_d = nc.dram_tensor("wpT", [E, E], F32, kind="ExternalInput")
    bqT_d = nc.dram_tensor("bqT", [E, R], F32, kind="ExternalInput")
    bkT_d = nc.dram_tensor("bkT", [E, R], F32, kind="ExternalInput")
    bvT_d = nc.dram_tensor("bvT", [E, R], F32, kind="ExternalInput")
    aqT_d = nc.dram_tensor("aqT", [R, E], F32, kind="ExternalInput")
    akT_d = nc.dram_tensor("akT", [R, E], F32, kind="ExternalInput")
    avT_d = nc.dram_tensor("avT", [R, E], F32, kind="ExternalInput")
    w1_d = nc.dram_tensor("w1", [NE * E, HID], F32, kind="ExternalInput")
    b1_d = nc.dram_tensor("b1", [NE, HID], F32, kind="ExternalInput")
    w2_d = nc.dram_tensor("w2", [NE * HID, E], F32, kind="ExternalInput")
    b2_d = nc.dram_tensor("b2", [NE, E], F32, kind="ExternalInput")
    wr_d = nc.dram_tensor("wr", [E, NE], F32, kind="ExternalInput")
    br_d = nc.dram_tensor("br", [NE, 1], F32, kind="ExternalInput")

    xs_d = nc.dram_tensor("xs", [E, NT], BF16, kind="Internal")
    z_d = nc.dram_tensor("z", [NT, E], BF16, kind="Internal")
    if debug:
        dxs_d = nc.dram_tensor("d_xs", [E, NT], BF16, kind="ExternalOutput")
        dz_d = nc.dram_tensor("d_z", [NT, E], BF16, kind="ExternalOutput")
        dm_d = nc.dram_tensor("d_m", [128, NE * NTC], F32, kind="ExternalOutput")
        dc_d = nc.dram_tensor("d_c", [128, E], F32, kind="ExternalOutput")
        dq_d = nc.dram_tensor("d_q", [128, EC * T], F32, kind="ExternalOutput")
        dk_d = nc.dram_tensor("d_k", [128, EC * T], F32, kind="ExternalOutput")
        dv_d = nc.dram_tensor("d_v", [128, (T // 128) * E], F32,
                              kind="ExternalOutput")
        do_d = nc.dram_tensor("d_o", [128, EC * T], F32, kind="ExternalOutput")

    es = ExitStack()

    def sb(name, shape, dt):
        return es.enter_context(nc.sbuf_tensor(name, shape, dt))

    ident_i = sb("ident_i", [128, 128], F32)
    ident_bf = sb("ident_bf", [128, 128], BF16)
    ident_f = sb("ident_f", [128, 128], F32)
    mtri = sb("mtri", [128, 128], F32)
    b1_fm = sb("b1_fm", [128, NE * HC], F32)
    rb1_fm = sb("rb1_fm", [128, NE * HC], F32)
    rb1_bf = sb("rb1_bf", [128, NE * HC], BF16)
    m_all = sb("m_all", [128, NE * NTC], F32)
    b2_bf = sb("b2_bf", [8, E], BF16)
    wr_sb = sb("wr_sb", [128, EC * NE], F32)
    wr_hi = sb("wr_hi", [128, EC * NE], BF16)
    wr_lo = sb("wr_lo", [128, EC * NE], BF16)
    br_col = sb("br_col", [8, 1], F32)
    ones1 = sb("ones1", [1, 128], BF16)
    ones8 = sb("ones8", [8, 1], BF16)
    csb = sb("csb", [1, E], BF16)
    cbc = sb("cbc", [128, E], F32)
    lg_sb = sb("lg_sb", [8, 128], F32)
    lgt = sb("lgt", [128, 8], F32)
    lnc = sb("lncols", [128, 8], F32)
    wes = ExitStack()
    w1sb = wes.enter_context(nc.sbuf_tensor("w1sb", [128, EC * HID], BF16))
    w2sb = wes.enter_context(nc.sbuf_tensor("w2sb", [128, HC * E], BF16))

    def sem(name):
        return Ctr(es.enter_context(nc.semaphore(name)))

    s_pre = sem("s_pre")
    g_pre = sem("g_pre")
    s_xin = [sem("s_xin0"), sem("s_xin1")]
    s_xsw = [sem("s_xsw0"), sem("s_xsw1")]
    s_xg = [sem("s_xg0"), sem("s_xg1")]
    g_w1 = sem("g_w1")
    g_w2 = sem("g_w2")
    g_z = [sem("g_z0"), sem("g_z1")]
    g_wf = sem("g_wf")
    g_ab = sem("g_ab")
    g_bb = sem("g_bb")
    s_xn3 = sem("s_xn3")
    s_zi = [sem("s_zi0"), sem("s_zi1")]
    s_xe = [sem("s_xe0"), sem("s_xe1")]
    s_out = [sem("s_out0"), sem("s_out1")]
    pe = sem("pe")
    ve = sem("ve")
    ae = sem("ae")
    ge = sem("ge")

    block = es.enter_context(nc.Block())
    p1es = ExitStack()
    b1_sb = p1es.enter_context(nc.sbuf_tensor("b1_sb", [8, HID], F32))

    with es:
        @block.sync
        def _(sy):
            sy.dma_start(b1_sb[:, :], b1_d[:, :]).then_inc(s_pre.h, 16)
            sy.dma_start(br_col[:, :], br_d[:, :]).then_inc(s_pre.h, 16)
            sy.dma_start(wr_sb[:, :],
                         bass.AP(wr_d, 0, [[NE, 128], [128 * NE, EC], [1, NE]])
                         ).then_inc(s_pre.h, 16)
            s_pre.n = 48

        @block.gpsimd
        def _(g):
            cg = Chain(g, ge)
            cg.iota(ident_i[:, :], pattern=[[-1, 128]], base=0,
                    channel_multiplier=1, allow_small_or_imprecise_dtypes=True)
            cg.memset(ones1[:, :], 1.0)
            cg.memset(ones8[:, :], 1.0)
            g.dma_start(b2_bf[:, :], b2_d[:, :]).then_inc(g_pre.h, 16)
            g_pre.n += 16
            for d in range(EC):
                g.dma_start(w1sb[:, d * HID:(d + 1) * HID],
                            w1_d[d * 128:(d + 1) * 128, :]).then_inc(g_w1.h, 16)
            g_w1.n += 16 * EC
            for h0 in range(0, HC, W2G):
                g.dma_start(
                    bass.AP(w2sb, h0 * E, [[HC * E, 128], [E, W2G], [1, E]]),
                    bass.AP(w2_d, h0 * 128 * E, [[E, 128], [128 * E, W2G], [1, E]]),
                ).then_inc(g_w2.h, 16)
            g_w2.n += 16 * (HC // W2G)

        @block.vector
        def _(v):
            cv = Chain(v, ve)
            ge.wait(v)
            cv.tensor_scalar(ident_bf[:, :], ident_i[:, :], 0.0, None,
                             ALU.is_equal)
            cv.tensor_scalar(ident_f[:, :], ident_i[:, :], 0.0, None,
                             ALU.is_equal)
            cv.tensor_scalar(mtri[:, :], ident_i[:, :], 0.0, NEG, ALU.is_lt,
                             ALU.mult)
            s_pre.wait(v)
            cv.tensor_copy(wr_hi[:, :], wr_sb[:, :])
            cv.tensor_tensor(wr_lo[:, :], wr_sb[:, :], wr_hi[:, :], ALU.subtract)

        with nc.psum_tensor("ptr8", [128, 8], F32) as ptr8:
            for c in range(HC):
                @block.tensor
                def _(t, c=c):
                    ve.wait(t)
                    pe.inc(t.transpose(ptr8[:, :],
                                       b1_sb[0:8, c * 128:(c + 1) * 128],
                                       ident_f[0:8, 0:8]))

                @block.vector
                def _(v, c=c):
                    cv = Chain(v, ve)
                    pe.wait(v)
                    cv.tensor_copy(
                        bass.AP(b1_fm, c, [[NE * HC, 128], [HC, 8]]),
                        ptr8[:, :])

            @block.vector
            def _(v):
                cv = Chain(v, ve)
                cv.tensor_scalar(rb1_fm[:, :], b1_fm[:, :], 0.0, None, ALU.max)
                cv.tensor_copy(rb1_bf[:, :], rb1_fm[:, :])

        def sb1(name, shape, dt):
            return p1es.enter_context(nc.sbuf_tensor(name, shape, dt))

        xbuf = [sb1("xbuf0", [128, E], F32), sb1("xbuf1", [128, E], F32)]
        xnf = [sb1("xnf0", [128, E], F32), sb1("xnf1", [128, E], F32)]
        xnh = sb1("xnh", [128, E], BF16)
        xnl = sb1("xnl", [128, E], BF16)
        stg = [sb1("stg0", [128, E], BF16), sb1("stg1", [128, E], BF16)]
        stl = sb1("stl", [128, E], BF16)

        p1_sqrt = {}
        p1_hilo = {}
        p1_tr = {}
        p1_lg = {}
        p1_stage_d = {}
        IVE = 1.0 / E
        with nc.psum_tensor("pst0", [128, 128], BF16) as pst0, \
             nc.psum_tensor("pst1", [128, 128], BF16) as pst1, \
             nc.psum_tensor("lgps", [8, 128], F32) as lgps, \
             nc.psum_tensor("lgtp", [128, 8], F32) as lgtp:
            pst = [pst0, pst1]
            for j in range(min(2, NTC)):
                @block.sync
                def _(sy, j=j):
                    sy.dma_start(xbuf[j % 2][:, :],
                                 x_d[j * 128:(j + 1) * 128, :]
                                 ).then_inc(s_xin[j % 2].h, 16)
                    s_xin[j % 2].n += 16

            for j in range(NTC):
                bi = j % 2

                @block.vector
                def _(v, j=j, bi=bi):
                    cv = Chain(v, ve)
                    s_xin[bi].wait(v)
                    if j >= 1:
                        ae.wait(v, p1_sqrt[j - 1])
                    xt = xbuf[bi]
                    cv.scalar_tensor_tensor(xnf[bi][:, :], xt[:, :], 1.0,
                                            xt[:, :], ALU.mult, ALU.mult,
                                            accum_out=lnc[:, 1:2])
                    cv.reduce_sum(lnc[:, 0:1], xt[:, :], axis=AXL.X)
                    cv.tensor_scalar(lnc[:, 2:3], lnc[:, 0:1], IVE, None,
                                     ALU.mult)
                    cv.tensor_scalar(lnc[:, 3:4], lnc[:, 1:2], IVE, None,
                                     ALU.mult)
                    cv.scalar_tensor_tensor(lnc[:, 4:5], lnc[:, 2:3],
                                            lnc[:, 2:3], lnc[:, 3:4],
                                            ALU.mult, ALU.subtract)
                    cv.tensor_scalar(lnc[:, 5:6], lnc[:, 4:5], -1.0, 1e-5,
                                     ALU.mult, ALU.add)

                @block.scalar
                def _(s, j=j):
                    ca = Chain(s, ae)
                    ve.wait(s)
                    ca.activation(lnc[:, 6:7], lnc[:, 5:6], AF.Sqrt)
                    p1_sqrt[j] = ae.n

                @block.vector
                def _(v, j=j, bi=bi):
                    cv = Chain(v, ve)
                    ae.wait(v, p1_sqrt[j])
                    if j >= 1:
                        pe.wait(v, p1_tr[j - 1])
                    cv.reciprocal(lnc[:, 7:8], lnc[:, 6:7])
                    cv.scalar_tensor_tensor(lnc[:, 4:5], lnc[:, 2:3], -1.0,
                                            lnc[:, 7:8], ALU.mult, ALU.mult)
                    cv.tensor_scalar(xnf[bi][:, :], xbuf[bi][:, :],
                                     lnc[:, 7:8], lnc[:, 4:5], ALU.mult,
                                     ALU.add)
                    cv.tensor_copy(xnh[:, :], xnf[bi][:, :])
                    cv.tensor_tensor(xnl[:, :], xnf[bi][:, :], xnh[:, :],
                                     ALU.subtract)
                    p1_hilo[j] = ve.n

                @block.sync
                def _(sy, j=j):
                    if j + 2 < NTC:
                        sy.wait_ge(ve.h, p1_hilo[j])
                        sy.dma_start(xbuf[j % 2][:, :],
                                     x_d[(j + 2) * 128:(j + 3) * 128, :]
                                     ).then_inc(s_xin[j % 2].h, 16)
                        s_xin[j % 2].n += 16

                @block.tensor
                def _(t, j=j):
                    ve.wait(t, p1_hilo[j])
                    if j >= 2:
                        s_xsw[j % 2].wait(t)
                    ab = ae.n
                    for c in range(EC):
                        w = ab + c - 1
                        if w >= 1:
                            t.wait_ge(ae.h, w)
                        pe.inc(t.transpose(pst[c % 2][:, :],
                                           xnh[:, c * 128:(c + 1) * 128],
                                           ident_bf[:, :]))
                    for c in range(EC):
                        if c < 2:
                            c2 = EC - 1 if (EC - 1) % 2 == c % 2 else EC - 2
                            w = ab + c2 + 1 if c2 >= 0 else ab
                        else:
                            w = ab + EC + c - 1
                        if w >= 1:
                            t.wait_ge(ae.h, w)
                        pe.inc(t.transpose(pst[c % 2][:, :],
                                           xnl[:, c * 128:(c + 1) * 128],
                                           ident_bf[:, :]))
                    p1_tr[j] = pe.n

                @block.scalar
                def _(s, j=j):
                    ca = Chain(s, ae)
                    base = pe.n - 2 * EC
                    for c in range(EC):
                        s.wait_ge(pe.h, base + c + 1)
                        ca.activation(stg[j % 2][:, c * 128:(c + 1) * 128],
                                      pst[c % 2][:, :], AF.Copy)
                    for c in range(EC):
                        s.wait_ge(pe.h, base + EC + c + 1)
                        ca.activation(stl[:, c * 128:(c + 1) * 128],
                                      pst[c % 2][:, :], AF.Copy)
                    p1_stage_d[j] = ae.n

                @block.tensor
                def _(t, j=j):
                    ae.wait(t, p1_stage_d[j])
                    if j >= 1:
                        ve.wait(t, p1_lg[j - 1])
                    first = True
                    for c in range(EC):
                        t.matmul(lgps[:, :], wr_hi[:, c * NE:(c + 1) * NE],
                                 stg[j % 2][:, c * 128:(c + 1) * 128],
                                 start=first, stop=False)
                        first = False
                        t.matmul(lgps[:, :], wr_lo[:, c * NE:(c + 1) * NE],
                                 stg[j % 2][:, c * 128:(c + 1) * 128],
                                 start=False, stop=False)
                        mm = t.matmul(lgps[:, :], wr_hi[:, c * NE:(c + 1) * NE],
                                      stl[:, c * 128:(c + 1) * 128],
                                      start=False, stop=(c == EC - 1))
                    pe.inc(mm)

                @block.vector
                def _(v, j=j):
                    cv = Chain(v, ve)
                    pe.wait(v)
                    cv.tensor_scalar(lg_sb[:, :], lgps[:, :], br_col[:, 0:1],
                                     None, ALU.add)
                    p1_lg[j] = ve.n

                @block.tensor
                def _(t, j=j):
                    ve.wait(t, p1_lg[j])
                    pe.inc(t.transpose(lgtp[:, :], lg_sb[:, :],
                                       ident_f[0:8, 0:8]))

                @block.vector
                def _(v, j=j):
                    cv = Chain(v, ve)
                    pe.wait(v)
                    cv.tensor_copy(lgt[:, :], lgtp[:, :])
                    cv.reduce_max(lnc[:, 0:1], lgt[:, :], axis=AXL.X)
                    cv.tensor_scalar(
                        bass.AP(m_all, j, [[NE * NTC, 128], [NTC, NE]]),
                        lgt[:, :], lnc[:, 0:1], None, ALU.is_ge)

                @block.sync
                def _(sy, j=j):
                    sy.wait_ge(ae.h, p1_stage_d[j])
                    sy.dma_start(
                        bass.AP(xs_d, j * 128,
                                [[NT, 128], [128 * NT, EC], [1, 128]]),
                        bass.AP(stg[j % 2], 0, [[E, 128], [128, EC], [1, 128]]),
                    ).then_inc(s_xsw[j % 2].h, 16)
                    s_xsw[j % 2].n += 16

        p1es.close()
        p2es = ExitStack()

        def sb2(name, shape, dt):
            return p2es.enter_context(nc.sbuf_tensor(name, shape, dt))

        xgb = [sb2("xgb0", [128, EC * TBL], BF16), sb2("xgb1", [128, EC * TBL], BF16)]
        h1p = sb2("h1p", [128, HC * TBL], BF16)
        scb = [sb2("scb0", [128, TBL], F32), sb2("scb1", [128, TBL], F32)]
        msc = [sb2("msc0", [128, E], BF16), sb2("msc1", [128, E], BF16)]

        p2_peA = {}
        p2_peB = {}
        p2_veh1 = {}
        p2_peC = {}
        msk_counts = []

        with nc.psum_tensor("h1ps0", [128, TBL], F32) as h1ps0, \
             nc.psum_tensor("h1ps1", [128, TBL], F32) as h1ps1, \
             nc.psum_tensor("zps0", [128, E], F32) as zps0, \
             nc.psum_tensor("zps1", [128, E], F32) as zps1, \
             nc.psum_tensor("cps", [1, E], F32) as cps:
            h1ps = [h1ps0, h1ps1]
            zps = [zps0, zps1]

            for gi0 in range(min(2, NE * NTB)):
                @block.sync
                def _(sy, gi0=gi0):
                    tb = gi0 % NTB
                    s_xsw[0].wait(sy)
                    s_xsw[1].wait(sy)
                    for c in range(EC):
                        sy.dma_start(
                            xgb[gi0 % 2][:, c * TBL:(c + 1) * TBL],
                            xs_d[c * 128:(c + 1) * 128, tb * TBL:(tb + 1) * TBL],
                        ).then_inc(s_xg[gi0 % 2].h, 16)
                    s_xg[gi0 % 2].n += 16 * EC

            for e in range(NE):
                for tb in range(NTB):
                    gi = e * NTB + tb

                    @block.tensor
                    def _(t, e=e, tb=tb, gi=gi):
                        s_xg[gi % 2].wait(t)
                        g_w1.wait(t, 16 * EC * (e + 1))
                        if gi == 0:
                            ve.wait(t)
                        ab = ae.n
                        for h in range(HC):
                            w = ab + h - 1
                            if w >= 1:
                                t.wait_ge(ae.h, w)
                            for d in range(EC):
                                mm = t.matmul(
                                    h1ps[h % 2][:, 0:TBL],
                                    w1sb[:, d * HID + h * 128:
                                         d * HID + (h + 1) * 128],
                                    xgb[gi % 2][:, d * TBL:(d + 1) * TBL],
                                    start=(d == 0), stop=(d == EC - 1))
                            pe.inc(mm)
                        p2_peA[(e, tb)] = pe.n

                    @block.scalar
                    def _(s, e=e, tb=tb):
                        ca = Chain(s, ae)
                        pb = p2_peA[(e, tb)] - HC
                        vb = ve.n
                        for h in range(HC):
                            s.wait_ge(pe.h, pb + h + 1)
                            w = vb + h - 1
                            if w >= 1:
                                s.wait_ge(ve.h, w)
                            ca.activation(
                                scb[h % 2][:, :], h1ps[h % 2][:, 0:TBL],
                                AF.Relu,
                                bias=b1_fm[:, e * HC + h:e * HC + h + 1])

                    @block.vector
                    def _(v, e=e, tb=tb, gi=gi):
                        cv = Chain(v, ve)
                        ab = ae.n - HC
                        if gi > 0:
                            v.wait_ge(pe.h, list(p2_peB.values())[-1])
                        for h in range(HC):
                            v.wait_ge(ae.h, ab + h + 1)
                            cv.tensor_scalar(
                                h1p[:, h * TBL:(h + 1) * TBL],
                                scb[h % 2][:, :],
                                rb1_fm[:, e * HC + h:e * HC + h + 1], None,
                                ALU.subtract)
                        p2_veh1[(e, tb)] = ve.n

                    @block.sync
                    def _(sy, e=e, tb=tb, gi=gi):
                        gi2 = gi + 2
                        if gi2 < NE * NTB:
                            tb2 = gi2 % NTB
                            sy.wait_ge(pe.h, p2_peA[(e, tb)])
                            for c in range(EC):
                                sy.dma_start(
                                    xgb[gi2 % 2][:, c * TBL:(c + 1) * TBL],
                                    xs_d[c * 128:(c + 1) * 128,
                                         tb2 * TBL:(tb2 + 1) * TBL],
                                ).then_inc(s_xg[gi2 % 2].h, 16)
                            s_xg[gi2 % 2].n += 16 * EC

                    @block.tensor
                    def _(t, e=e, tb=tb, gi=gi):
                        ve.wait(t, p2_veh1[(e, tb)])
                        g_w2.wait(t, 16 * (HC // W2G) * (e + 1))
                        for tc in range(TCB):
                            gtc = gi * TCB + tc
                            prev = gtc - 2
                            if prev >= 0:
                                if prev >= gi * TCB:
                                    t.wait_ge(ve.h,
                                              p2_veh1[(e, tb)] + (prev - gi * TCB) + 1)
                                else:
                                    t.wait_ge(ve.h, msk_counts[prev])
                            for h in range(HC):
                                for half in range(EH):
                                    mm = t.matmul(
                                        zps[tc % 2][:, half * EW:(half + 1) * EW],
                                        h1p[:, h * TBL + tc * 128:
                                             h * TBL + tc * 128 + 128],
                                        w2sb[:, h * E + half * EW:
                                             h * E + (half + 1) * EW],
                                        start=(h == 0), stop=(h == HC - 1))
                            pe.inc(mm)
                            p2_peB[(e, tb, tc)] = pe.n

                    @block.vector
                    def _(v, e=e, tb=tb, gi=gi):
                        cv = Chain(v, ve)
                        for tc in range(TCB):
                            gtc = gi * TCB + tc
                            jt = tb * TCB + tc
                            v.wait_ge(pe.h, p2_peB[(e, tb, tc)])
                            if gtc >= 2:
                                v.wait_ge(g_z[gtc % 2].h, 16 * (gtc // 2))
                            cv.tensor_scalar(
                                msc[tc % 2][:, :], zps[tc % 2][:, :],
                                m_all[:, e * NTC + jt:e * NTC + jt + 1], None,
                                ALU.mult)
                            msk_counts.append(ve.n)

                    @block.gpsimd
                    def _(g, e=e, tb=tb, gi=gi):
                        for tc in range(TCB):
                            gtc = gi * TCB + tc
                            jt = tb * TCB + tc
                            g.wait_ge(ve.h, msk_counts[gtc])
                            if e > 0 and tb == 0 and tc == 0:
                                g_z[0].wait(g)
                                g_z[1].wait(g)
                            kw = {} if e == 0 else dict(accum_op=ALU.add)
                            g.dma_start(z_d[jt * 128:(jt + 1) * 128, :],
                                        msc[tc % 2][:, :], **kw
                                        ).then_inc(g_z[tc % 2].h, 16)
                            g_z[tc % 2].n += 16

                @block.tensor
                def _(t, e=e):
                    for h in range(HC):
                        for half in range(EH):
                            mm = t.matmul(
                                cps[0:1, half * EW:(half + 1) * EW],
                                rb1_bf[:, e * HC + h:e * HC + h + 1],
                                w2sb[:, h * E + half * EW:
                                     h * E + (half + 1) * EW],
                                start=(e == 0 and h == 0), stop=False,
                                skip_group_check=True)
                    pe.inc(mm)
                    p2_peC[e] = pe.n

                @block.gpsimd
                def _(g, e=e):
                    if e + 1 < NE:
                        g.wait_ge(pe.h, p2_peA[(e, NTB - 1)])
                        for d in range(EC):
                            g.dma_start(
                                w1sb[:, d * HID:(d + 1) * HID],
                                w1_d[(e + 1) * E + d * 128:
                                     (e + 1) * E + (d + 1) * 128, :]
                            ).then_inc(g_w1.h, 16)
                        g_w1.n += 16 * EC
                        g.wait_ge(pe.h, p2_peC[e])
                        for h0 in range(0, HC, W2G):
                            g.dma_start(
                                bass.AP(w2sb, h0 * E,
                                        [[HC * E, 128], [E, W2G], [1, E]]),
                                bass.AP(w2_d, ((e + 1) * HID + h0 * 128) * E,
                                        [[E, 128], [128 * E, W2G], [1, E]]),
                            ).then_inc(g_w2.h, 16)
                        g_w2.n += 16 * (HC // W2G)

            @block.tensor
            def _(t):
                g_pre.wait(t, 16)
                for half in range(EH):
                    mm = t.matmul(cps[0:1, half * EW:(half + 1) * EW],
                                  ones8[:, 0:1],
                                  b2_bf[:, half * EW:(half + 1) * EW],
                                  start=False, stop=True,
                                  skip_group_check=True)
                pe.inc(mm)

            @block.vector
            def _(v):
                cv = Chain(v, ve)
                pe.wait(v)
                cv.tensor_copy(csb[:, :], cps[0:1, :])

        with nc.psum_tensor("cbps", [128, E], F32) as cbps:
            @block.tensor
            def _(t):
                ve.wait(t)
                for half in range(EH):
                    mm = t.matmul(cbps[:, half * EW:(half + 1) * EW],
                                  ones1[:, :], csb[0:1, half * EW:(half + 1) * EW],
                                  start=True, stop=True)
                pe.inc(mm)

            @block.vector
            def _(v):
                cv = Chain(v, ve)
                pe.wait(v)
                cv.tensor_copy(cbc[:, :], cbps[:, :])

        if debug:
            @block.sync
            def _(sy):
                g_z[0].wait(sy)
                g_z[1].wait(sy)
                sy.dma_start(dz_d[:, :], z_d[:, :]).then_inc(s_pre.h, 16)
                sy.dma_start(dxs_d[:, :], xs_d[:, :]).then_inc(s_pre.h, 16)
                ve.wait(sy)
                sy.dma_start(dm_d[:, :], m_all[:, :]).then_inc(s_pre.h, 16)
                sy.dma_start(dc_d[:, :], cbc[:, :]).then_inc(s_pre.h, 16)
                s_pre.n += 64

        p2es.close()
        wes.close()
        jbc = sb("jbc", [128, T], F32)
        k_fm = sb("k_fm", [128, EC * T], BF16)
        v_tm = sb("v_tm", [128, (T // 128) * E], BF16)
        q_fm = sb("q_fm", [128, EC * T], BF16)
        o_fm = sb("o_fm", [128, EC * T], BF16)
        wbuf = sb("wbuf", [128, EC * E], BF16)
        ab_sb = sb("ab_sb", [8, E], BF16)
        bb_sb = sb("bb_sb", [128, EC * R], BF16)
        u_sb = sb("u_sb", [8, T], BF16)

        p3_pe = {}
        p3_pe[(-1, 'end')] = pe.n
        p3_uve = {}
        p3_sv = {}
        prev_sv_last = [None]

        @block.gpsimd
        def _(g):
            cg = Chain(g, ge)
            g.wait_ge(pe.h, p3_pe[(-1, 'end')])
            cg.iota(jbc[:, :], pattern=[[1, T]], base=0, channel_multiplier=0,
                    allow_small_or_imprecise_dtypes=True)
        ge_jbc = ge.n

        for sq in range(NSEQ):
            t0 = sq * T
            pAes = ExitStack()
            xn3 = pAes.enter_context(
                nc.sbuf_tensor("xn3_" + str(sq), [128, EC * T], BF16))

            @block.sync
            def _(sy, sq=sq, t0=t0, xn3=xn3):
                if sq > 0:
                    pe.wait(sy)
                    sy.wait_ge(ve.h, prev_sv_last[0])
                    s_out[0].wait(sy)
                    s_out[1].wait(sy)
                else:
                    s_xsw[0].wait(sy)
                    s_xsw[1].wait(sy)
                    sy.wait_ge(pe.h, p3_pe[(-1, 'end')])
                for c in range(EC):
                    sy.dma_start(xn3[:, c * T:(c + 1) * T],
                                 xs_d[c * 128:(c + 1) * 128, t0:t0 + T]
                                 ).then_inc(s_xn3.h, 16)
                s_xn3.n += 16 * EC

            def emit_u(bd, sq, xn3, first):
                @block.gpsimd
                def _(g):
                    w = p3_pe.get((sq, 'wfree'),
                                  p3_pe.get((sq - 1, 'end'), 0))
                    if w:
                        g.wait_ge(pe.h, w)
                    g.dma_start(
                        bb_sb[:, :],
                        bass.AP(bd, 0, [[R, 128], [128 * R, EC], [1, R]]),
                    ).then_inc(g_bb.h, 16)
                    g_bb.n += 16

                with nc.psum_tensor("ups_u" + str(sq) + str(g_bb.n),
                                    [8, T], F32) as ups:
                    @block.tensor
                    def _(t):
                        g_bb.wait(t)
                        s_xn3.wait(t)
                        ae.wait(t)
                        if first:
                            ve.wait(t)
                        if sq > 0 and prev_sv_last[0] is not None:
                            t.wait_ge(ve.h, prev_sv_last[0])
                        w = p3_pe.get((sq, 'wfree'), 0)
                        if w:
                            t.wait_ge(ve.h, p3_uve[(sq, 'last')])
                        for c in range(EC):
                            for nb in range(NPB):
                                mm = t.matmul(
                                    ups[:, nb * PB:(nb + 1) * PB],
                                    bb_sb[:, c * R:(c + 1) * R],
                                    xn3[:, c * T + nb * PB:
                                        c * T + (nb + 1) * PB],
                                    start=(c == 0), stop=(c == EC - 1))
                        pe.inc(mm)

                    @block.vector
                    def _(v):
                        cv = Chain(v, ve)
                        pe.wait(v)
                        cv.tensor_scalar(u_sb[:, :], ups[:, :], LORA_S, None,
                                         ALU.mult)
                        p3_uve[(sq, 'last')] = ve.n

            def proj_fm(wT_d, aT_d, bT_d, dest, scale, sq, tag, xn3,
                        first=False):
                emit_u(bT_d, sq, xn3, first)

                @block.gpsimd
                def _(g):
                    w = p3_pe.get((sq, 'wfree'),
                                  p3_pe.get((sq - 1, 'end'), 0))
                    if w:
                        g.wait_ge(pe.h, w)
                    for c in range(EC):
                        g.dma_start(wbuf[:, c * E:(c + 1) * E],
                                    wT_d[c * 128:(c + 1) * 128, :]
                                    ).then_inc(g_wf.h, 16)
                    g_wf.n += 16 * EC
                    g.dma_start(ab_sb[:, :], aT_d[:, :]).then_inc(g_ab.h, 16)
                    g_ab.n += 16

                with nc.psum_tensor("psA_" + tag + str(sq), [128, 4 * 512],
                                    F32) as psA:
                    for ic in range(EC):
                        @block.tensor
                        def _(t, ic=ic):
                            g_wf.wait(t)
                            g_ab.wait(t)
                            ve.wait(t, p3_uve[(sq, 'last')])
                            ae.wait(t)
                            for ec in range(EC):
                                for nb in range(NPB):
                                    t.matmul(
                                        psA[:, (nb % 4) * 512:
                                            (nb % 4) * 512 + PB],
                                        wbuf[:, ec * E + ic * 128:
                                             ec * E + ic * 128 + 128],
                                        xn3[:, ec * T + nb * PB:
                                            ec * T + (nb + 1) * PB],
                                        start=(ec == 0), stop=False)
                            for nb in range(NPB):
                                mm = t.matmul(
                                    psA[:, (nb % 4) * 512:(nb % 4) * 512 + PB],
                                    ab_sb[0:8, ic * 128:(ic + 1) * 128],
                                    u_sb[0:8, nb * PB:(nb + 1) * PB],
                                    start=False, stop=True)
                            pe.inc(mm)

                        @block.scalar
                        def _(s, ic=ic):
                            ca = Chain(s, ae)
                            pe.wait(s)
                            for nb in range(NPB):
                                ca.activation(
                                    dest[:, ic * T + nb * PB:
                                         ic * T + (nb + 1) * PB],
                                    psA[:, (nb % 4) * 512:(nb % 4) * 512 + PB],
                                    AF.Copy, scale=scale)
                p3_pe[(sq, 'wfree')] = pe.n

            proj_fm(wkT_d, akT_d, bkT_d, k_fm, 1.0, sq, "k", xn3,
                    first=(sq == 0))

            emit_u(bvT_d, sq, xn3, False)

            @block.gpsimd
            def _(g, sq=sq):
                g.wait_ge(pe.h, p3_pe[(sq, 'wfree')])
                for c in range(EC):
                    g.dma_start(wbuf[:, c * E:(c + 1) * E],
                                wvT_d[c * 128:(c + 1) * 128, :]
                                ).then_inc(g_wf.h, 16)
                g_wf.n += 16 * EC
                g.dma_start(ab_sb[:, :], avT_d[:, :]).then_inc(g_ab.h, 16)
                g_ab.n += 16

            with nc.psum_tensor("psV" + str(sq), [128, E], F32) as psV:
                for tc in range(T // 128):
                    @block.tensor
                    def _(t, tc=tc, sq=sq):
                        g_wf.wait(t)
                        g_ab.wait(t)
                        ve.wait(t, p3_uve[(sq, 'last')])
                        ae.wait(t)
                        for ec in range(EC):
                            for half in range(EH):
                                t.matmul(
                                    psV[:, half * EW:(half + 1) * EW],
                                    xn3[:, ec * T + tc * 128:
                                        ec * T + tc * 128 + 128],
                                    wbuf[:, ec * E + half * EW:
                                         ec * E + (half + 1) * EW],
                                    start=(ec == 0), stop=False)
                        for half in range(EH):
                            mm = t.matmul(
                                psV[:, half * EW:(half + 1) * EW],
                                u_sb[0:8, tc * 128: tc * 128 + 128],
                                ab_sb[0:8, half * EW:(half + 1) * EW],
                                start=False, stop=True)
                        pe.inc(mm)

                    @block.scalar
                    def _(s, tc=tc):
                        ca = Chain(s, ae)
                        pe.wait(s)
                        ca.activation(v_tm[:, tc * E:(tc + 1) * E],
                                      psV[:, :], AF.Copy)
            p3_pe[(sq, 'wfree')] = pe.n

            proj_fm(wqT_d, aqT_d, bqT_d, q_fm, SCL, sq, "q", xn3)
            proj_ae_done = ae.n

            @block.gpsimd
            def _(g, sq=sq):
                g.wait_ge(pe.h, p3_pe[(sq, 'wfree')])
                for c in range(EC):
                    g.dma_start(wbuf[:, c * E:(c + 1) * E],
                                wpT_d[c * 128:(c + 1) * 128, :]
                                ).then_inc(g_wf.h, 16)
                g_wf.n += 16 * EC

            pAes.close()
            pBes = ExitStack()

            def sb3(name, shape, dt):
                return pBes.enter_context(
                    nc.sbuf_tensor(name + "_" + str(sq), shape, dt))

            colfac = sb3("colfac", [128, T], BF16)
            praw = sb3("praw", [128, T], BF16)
            pnrm = praw
            pt_sb = sb3("pt_sb", [128, T], BF16)
            ztb = [sb3("ztb0", [128, E], BF16), sb3("ztb1", [128, E], BF16)]
            xtb = [sb3("xtb0", [128, E], F32), sb3("xtb1", [128, E], F32)]
            gts = sb3("gts", [128, E], BF16)
            otb = [sb3("otb0", [128, E], F32), sb3("otb1", [128, E], F32)]

            with nc.psum_tensor("sbig" + str(sq), [128, SW], F32) as sbig, \
                 nc.psum_tensor("ptp0" + str(sq), [128, 128], BF16) as ptp0, \
                 nc.psum_tensor("ptp1" + str(sq), [128, 128], BF16) as ptp1, \
                 nc.psum_tensor("ops0" + str(sq), [64, 128], F32) as ops0, \
                 nc.psum_tensor("ops1" + str(sq), [64, 128], F32) as ops1:
                ptp = [ptp0, ptp1]
                ops = [ops0, ops1]
                it = 0
                p3_exp = {}
                p3_oev = {}
                p3_pnm = {}
                last_user = {0: None, 1: None}
                for h in range(H):
                    hk = h // 2
                    hb = (h % 2) * 64

                    @block.scalar
                    def _(s, h=h):
                        ca = Chain(s, ae)
                        ge.wait(s, ge_jbc)
                        if h >= 1:
                            ve.wait(s)
                        ca.activation(colfac[:, :], jbc[:, :], AF.Exp,
                                      scale=-slopes[h])

                    for qt in range(QTN):
                        q0 = qt * 128
                        W = (qt + 1) * 128
                        covS = min(((q0 + 128 + 511) // 512) * 512, T)
                        small = covS <= SW // 2
                        soff = (SW // 2) * (it % 2) if small else 0
                        regions = [it % 2] if small else [0, 1]
                        deps = [last_user[r] for r in regions
                                if last_user[r] is not None]

                        @block.tensor
                        def _(t, it=it, soff=soff, covS=covS, hk=hk, hb=hb,
                              q0=q0, deps=tuple(deps), sq=sq,
                              proj_ae_done=proj_ae_done):
                            if deps:
                                t.wait_ge(ae.h, p3_exp[max(deps)])
                            elif sq > 0 and prev_sv_last[0] is not None:
                                t.wait_ge(ve.h, prev_sv_last[0])
                            if it == 0:
                                t.wait_ge(ae.h, proj_ae_done)
                            nkc = (covS + 511) // 512
                            for nk in range(nkc):
                                n0 = nk * 512
                                n1 = min(n0 + 512, covS)
                                mm = t.matmul(
                                    sbig[:, soff + n0: soff + n1],
                                    q_fm[hb:hb + 64,
                                         hk * T + q0: hk * T + q0 + 128],
                                    k_fm[hb:hb + 64, hk * T + n0: hk * T + n1],
                                    start=True, stop=True)
                            pe.inc(mm)

                        for r in regions:
                            last_user[r] = it

                        @block.vector
                        def _(v, it=it, soff=soff, q0=q0, W=W):
                            cv = Chain(v, ve)
                            pe.wait(v)
                            if it >= 1:
                                ae.wait(v, p3_exp[it - 1])
                            cv.tensor_tensor(
                                sbig[:, soff + q0: soff + q0 + 128],
                                mtri[:, :],
                                sbig[:, soff + q0: soff + q0 + 128],
                                ALU.add)
                            cv.reduce_max(lnc[:, 0:1], sbig[:, soff:soff + W],
                                          axis=AXL.X, negate=True)

                        @block.scalar
                        def _(s, it=it, soff=soff, W=W):
                            ca = Chain(s, ae)
                            ve.wait(s)
                            ca.activation(praw[:, 0:W], sbig[:, soff:soff + W],
                                          AF.Exp, bias=lnc[:, 0:1])
                            p3_exp[it] = ae.n

                        @block.vector
                        def _(v, it=it, W=W):
                            cv = Chain(v, ve)
                            ae.wait(v, p3_exp[it])
                            if it >= 1:
                                ge.wait(v, p3_pnm[it - 1])
                            cv.scalar_tensor_tensor(pnrm[:, 0:W], praw[:, 0:W],
                                                    1.0, colfac[:, 0:W],
                                                    ALU.mult, ALU.mult,
                                                    accum_out=lnc[:, 1:2])
                            cv.reciprocal(lnc[:, 2:3], lnc[:, 1:2])

                        @block.gpsimd
                        def _(g, it=it, W=W):
                            cg = Chain(g, ge)
                            ve.wait(g)
                            cg.tensor_scalar(pnrm[:, 0:W], pnrm[:, 0:W],
                                             lnc[:, 2:3], None, ALU.mult)
                            p3_pnm[it] = ge.n

                        @block.tensor
                        def _(t, it=it, qt=qt):
                            ge.wait(t, p3_pnm[it])
                            ab = ae.n
                            for kc in range(qt + 1):
                                w = ab + kc - 1
                                if w >= 1:
                                    t.wait_ge(ae.h, w)
                                pe.inc(t.transpose(
                                    ptp[kc % 2][:, :],
                                    pnrm[:, kc * 128:(kc + 1) * 128],
                                    ident_bf[:, :]))

                        @block.scalar
                        def _(s, it=it, qt=qt):
                            ca = Chain(s, ae)
                            pb = pe.n - (qt + 1)
                            for kc in range(qt + 1):
                                s.wait_ge(pe.h, pb + kc + 1)
                                ca.activation(
                                    pt_sb[:, kc * 128:(kc + 1) * 128],
                                    ptp[kc % 2][:, :], AF.Copy)

                        @block.tensor
                        def _(t, it=it, qt=qt, h=h):
                            ab = ae.n - (qt + 1)
                            if it >= 2:
                                t.wait_ge(ae.h, p3_oev[it - 2])
                            for kc in range(qt + 1):
                                t.wait_ge(ae.h, ab + kc + 1)
                                mm = t.matmul(
                                    ops[it % 2][:, :],
                                    v_tm[:, kc * E + h * DH:
                                         kc * E + (h + 1) * DH],
                                    pt_sb[:, kc * 128:(kc + 1) * 128],
                                    start=(kc == 0), stop=(kc == qt))
                            pe.inc(mm)

                        @block.scalar
                        def _(s, it=it, qt=qt, hk=hk, hb=hb):
                            ca = Chain(s, ae)
                            pe.wait(s)
                            ca.activation(
                                o_fm[hb:hb + 64,
                                     hk * T + qt * 128: hk * T + qt * 128 + 128],
                                ops[it % 2][:, :], AF.Copy)
                            p3_oev[it] = ae.n

                        it += 1

                for gi0 in range(min(2, QTN)):
                    @block.sync
                    def _(sy, gi0=gi0, sq=sq, t0=t0):
                        g_z[0].wait(sy)
                        g_z[1].wait(sy)
                        pe.wait(sy)
                        sy.dma_start(
                            ztb[gi0 % 2][:, :],
                            z_d[t0 + gi0 * 128: t0 + (gi0 + 1) * 128, :]
                        ).then_inc(s_zi[gi0 % 2].h, 16)
                        s_zi[gi0 % 2].n += 16
                        sy.dma_start(
                            xtb[gi0 % 2][:, :],
                            x_d[t0 + gi0 * 128: t0 + (gi0 + 1) * 128, :]
                        ).then_inc(s_xe[gi0 % 2].h, 16)
                        s_xe[gi0 % 2].n += 16

                for qt in range(QTN):
                    @block.tensor
                    def _(t, qt=qt, sq=sq):
                        g_wf.wait(t)
                        if qt == 0:
                            ae.wait(t)
                        if qt >= 2:
                            ve.wait(t, p3_sv[(sq, qt - 2)])
                        for dc in range(EC):
                            for half in range(EH):
                                mm = t.matmul(
                                    sbig[:, (qt % 2) * E + half * EW:
                                         (qt % 2) * E + (half + 1) * EW],
                                    o_fm[:, dc * T + qt * 128:
                                         dc * T + qt * 128 + 128],
                                    wbuf[:, dc * E + half * EW:
                                         dc * E + (half + 1) * EW],
                                    start=(dc == 0), stop=(dc == EC - 1))
                        pe.inc(mm)

                    @block.vector
                    def _(v, qt=qt, sq=sq):
                        cv = Chain(v, ve)
                        bi = qt % 2
                        pe.wait(v)
                        s_zi[bi].wait(v)
                        gidx = sq * QTN + qt
                        if gidx >= 2:
                            v.wait_ge(s_out[gidx % 2].h, 16 * (gidx // 2))
                        cv.tensor_tensor(otb[bi][:, :], ztb[bi][:, :],
                                         cbc[:, :], ALU.add)

                    @block.scalar
                    def _(s, qt=qt):
                        ca = Chain(s, ae)
                        ve.wait(s)
                        ca.activation(gts[:, :], otb[qt % 2][:, :], AF.Silu)

                    @block.vector
                    def _(v, qt=qt, sq=sq):
                        cv = Chain(v, ve)
                        bi = qt % 2
                        ae.wait(v)
                        s_xe[bi].wait(v)
                        cv.tensor_tensor(otb[bi][:, :],
                                         sbig[:, bi * E:(bi + 1) * E],
                                         gts[:, :], ALU.mult)
                        cv.tensor_tensor(otb[bi][:, :], otb[bi][:, :],
                                         xtb[bi][:, :], ALU.add)
                        p3_sv[(sq, qt)] = ve.n

                    @block.sync
                    def _(sy, qt=qt, sq=sq, t0=t0):
                        bi = qt % 2
                        sy.wait_ge(ve.h, p3_sv[(sq, qt)])
                        sy.dma_start(
                            out_d[t0 + qt * 128: t0 + (qt + 1) * 128, :],
                            otb[bi][:, :]).then_inc(s_out[(sq * QTN + qt) % 2].h, 16)
                        s_out[(sq * QTN + qt) % 2].n += 16
                        gi2 = qt + 2
                        if gi2 < QTN:
                            sy.dma_start(
                                ztb[gi2 % 2][:, :],
                                z_d[t0 + gi2 * 128: t0 + (gi2 + 1) * 128, :]
                            ).then_inc(s_zi[gi2 % 2].h, 16)
                            s_zi[gi2 % 2].n += 16
                            sy.dma_start(
                                xtb[gi2 % 2][:, :],
                                x_d[t0 + gi2 * 128: t0 + (gi2 + 1) * 128, :]
                            ).then_inc(s_xe[gi2 % 2].h, 16)
                            s_xe[gi2 % 2].n += 16

                prev_sv_last[0] = p3_sv[(sq, QTN - 1)]
            p3_pe[(sq, 'end')] = pe.n
            pBes.close()

            if debug and sq == NSEQ - 1:
                @block.gpsimd
                def _(g):
                    ae.wait(g)
                    ve.wait(g)
                    for (dd, ss) in ((dq_d, q_fm), (dk_d, k_fm), (dv_d, v_tm),
                                     (do_d, o_fm)):
                        g.dma_start(dd[:, :], ss[:, :]).then_inc(g_pre.h, 16)
                        g_pre.n += 16

        @block.sync
        def _(sy):
            sy.wait_ge(s_out[0].h, s_out[0].n)
            sy.wait_ge(s_out[1].h, s_out[1].n)
            if debug:
                sy.wait_ge(s_pre.h, s_pre.n)
                sy.wait_ge(g_pre.h, g_pre.n)

    return nc


_NC_CACHE = {}


def _get_nc(key, cfg, debug=False):
    if key not in _NC_CACHE:
        _NC_CACHE[key] = build(cfg, debug=debug)
    return _NC_CACHE[key]


def make_in_maps(inputs, cfg, ncore=8):
    E, T, NSEQ, HID = cfg["E"], cfg["T"], cfg["NSEQ"], cfg["HID"]
    x = np.asarray(inputs["L0_attn_x"], np.float32)
    B = x.shape[0]
    per = B // ncore
    assert per * ncore == B and per == NSEQ

    def t(a):
        return np.ascontiguousarray(np.asarray(a, np.float32).T)

    com = {
        "wqT": t(inputs["L0_attn_Wq"]), "wkT": t(inputs["L0_attn_Wk"]),
        "wvT": t(inputs["L0_attn_Wv"]), "wpT": t(inputs["L0_attn_Wp"]),
        "bqT": t(inputs["L0_attn_Bq"]), "bkT": t(inputs["L0_attn_Bk"]),
        "bvT": t(inputs["L0_attn_Bv"]),
        "aqT": t(inputs["L0_attn_Aq"]), "akT": t(inputs["L0_attn_Ak"]),
        "avT": t(inputs["L0_attn_Av"]),
        "w1": np.ascontiguousarray(
            np.asarray(inputs["L0_ffn_W1"], np.float32).reshape(-1, HID)),
        "b1": np.asarray(inputs["L0_ffn_b1"], np.float32),
        "w2": np.ascontiguousarray(
            np.asarray(inputs["L0_ffn_W2"], np.float32).reshape(-1, E)),
        "b2": np.asarray(inputs["L0_ffn_b2"], np.float32),
        "wr": np.asarray(inputs["L0_ffn_Wr"], np.float32),
        "br": np.ascontiguousarray(
            np.asarray(inputs["L0_ffn_br"], np.float32).reshape(-1, 1)),
    }
    in_maps = []
    for c in range(ncore):
        m = dict(com)
        m["x"] = np.ascontiguousarray(
            x[c * per:(c + 1) * per].reshape(per * T, E))
        in_maps.append(m)
    return in_maps


def kernel(**inputs):
    cfg = CFG_FULL
    NCORE = 8
    T, E = cfg["T"], cfg["E"]
    in_maps = make_in_maps(inputs, cfg, NCORE)
    per = cfg["NSEQ"]
    nc = _get_nc("full", cfg)
    res = run_bass_kernel_spmd(nc, in_maps, core_ids=list(range(NCORE)))
    out = np.concatenate(
        [np.asarray(r["out"]).reshape(per, T, E) for r in res.results], axis=0)
    return out.astype(np.float32)


# version 03
# speedup vs baseline: 1.5305x; 1.5305x over previous
import math
import numpy as np
import ml_dtypes
from contextlib import ExitStack
import concourse.bass as bass
import concourse.mybir as mybir
from concourse.bass_utils import run_bass_kernel_spmd

F32 = mybir.dt.float32
BF16 = mybir.dt.bfloat16
AF = mybir.ActivationFunctionType
ALU = mybir.AluOpType
AXL = mybir.AxisListType

NEG = -1.0e30

CFG_FULL = dict(E=1024, H=16, HID=4096, NE=8, T=2048, NSEQ=2, R=8)
CFG_SIM = dict(E=128, H=2, HID=512, NE=8, T=256, NSEQ=2, R=8)

_CHAIN_OPS = {
    "activation", "copy", "add", "mul", "sqrt", "square", "sign", "memset",
    "memzero", "tensor_copy", "tensor_tensor", "tensor_add", "tensor_sub",
    "tensor_mul", "tensor_max", "tensor_scalar", "tensor_single_scalar",
    "tensor_scalar_add", "tensor_scalar_sub", "tensor_scalar_mul",
    "tensor_scalar_max", "tensor_scalar_min", "tensor_relu", "tensor_reduce",
    "reduce_max", "reduce_sum", "reciprocal", "select", "copy_predicated",
    "tensor_tensor_reduce", "scalar_tensor_tensor", "tensor_tensor_scan",
    "iota", "bn_stats", "bn_aggr", "max", "max_index", "stream_shuffle",
    "affine_select",
}


class Ctr:

    def __init__(self, h):
        self.h = h
        self.n = 0

    def inc(self, instr, k=1):
        instr.then_inc(self.h, k)
        self.n += k
        return self.n

    def wait(self, eng, n=None):
        v = self.n if n is None else n
        if v > 0:
            eng.wait_ge(self.h, v)


class Chain:

    def __init__(self, eng, ctr):
        self._e = eng
        self._c = ctr

    def __getattr__(self, name):
        e = self._e
        a = getattr(e, name)
        if name not in _CHAIN_OPS:
            return a
        c = self._c

        def wrapped(*args, **kw):
            if c.n > 0:
                e.wait_ge(c.h, c.n)
            instr = a(*args, **kw)
            c.inc(instr)
            return instr

        return wrapped


def build(cfg, debug=False):
    E, H, HID, NE, T, NSEQ, R = (cfg["E"], cfg["H"], cfg["HID"], cfg["NE"],
                                 cfg["T"], cfg["NSEQ"], cfg["R"])
    DH = 64
    assert E == H * DH
    EC = E // 128
    HC = HID // 128
    NT = T * NSEQ
    NTC = NT // 128
    TBL = min(512, NT)
    NTB = NT // TBL
    TCB = TBL // 128
    QTN = T // 128
    PB = min(512, T)
    NPB = T // PB
    assert NPB <= 4
    EW = min(512, E)
    EH = E // EW
    SCL = 0.125
    LORA_S = 2.0
    SW = min(2 * T, 2048)
    start = 2 ** (-2 ** (-(math.log2(H) - 3)))
    slopes = [start * start ** i for i in range(H)]
    W2G = 4 if HC % 4 == 0 else 1

    nc = bass.Bass("TRN2", target_bir_lowering=False)

    x_d = nc.dram_tensor("x", [NT, E], F32, kind="ExternalInput")
    out_d = nc.dram_tensor("out", [NT, E], F32, kind="ExternalOutput")
    wqT_d = nc.dram_tensor("wqT", [E, E], F32, kind="ExternalInput")
    wkT_d = nc.dram_tensor("wkT", [E, E], F32, kind="ExternalInput")
    wvT_d = nc.dram_tensor("wvT", [E, E], F32, kind="ExternalInput")
    wpT_d = nc.dram_tensor("wpT", [E, E], F32, kind="ExternalInput")
    bqT_d = nc.dram_tensor("bqT", [E, R], F32, kind="ExternalInput")
    bkT_d = nc.dram_tensor("bkT", [E, R], F32, kind="ExternalInput")
    bvT_d = nc.dram_tensor("bvT", [E, R], F32, kind="ExternalInput")
    aqT_d = nc.dram_tensor("aqT", [R, E], F32, kind="ExternalInput")
    akT_d = nc.dram_tensor("akT", [R, E], F32, kind="ExternalInput")
    avT_d = nc.dram_tensor("avT", [R, E], F32, kind="ExternalInput")
    w1_d = nc.dram_tensor("w1", [NE * E, HID], F32, kind="ExternalInput")
    b1_d = nc.dram_tensor("b1", [NE, HID], F32, kind="ExternalInput")
    w2_d = nc.dram_tensor("w2", [NE * HID, E], F32, kind="ExternalInput")
    b2_d = nc.dram_tensor("b2", [NE, E], F32, kind="ExternalInput")
    wr_d = nc.dram_tensor("wr", [E, NE], F32, kind="ExternalInput")
    br_d = nc.dram_tensor("br", [NE, 1], F32, kind="ExternalInput")

    xs_d = nc.dram_tensor("xs", [E, NT], BF16, kind="Internal")
    z_d = nc.dram_tensor("z", [NT, E], BF16, kind="Internal")
    if debug:
        dxs_d = nc.dram_tensor("d_xs", [E, NT], BF16, kind="ExternalOutput")
        dz_d = nc.dram_tensor("d_z", [NT, E], BF16, kind="ExternalOutput")
        dm_d = nc.dram_tensor("d_m", [128, NE * NTC], F32, kind="ExternalOutput")
        dc_d = nc.dram_tensor("d_c", [128, E], F32, kind="ExternalOutput")
        dq_d = nc.dram_tensor("d_q", [128, EC * T], F32, kind="ExternalOutput")
        dk_d = nc.dram_tensor("d_k", [128, EC * T], F32, kind="ExternalOutput")
        dv_d = nc.dram_tensor("d_v", [128, (T // 128) * E], F32,
                              kind="ExternalOutput")
        do_d = nc.dram_tensor("d_o", [128, EC * T], F32, kind="ExternalOutput")

    es = ExitStack()

    def sb(name, shape, dt):
        return es.enter_context(nc.sbuf_tensor(name, shape, dt))

    ident_i = sb("ident_i", [128, 128], F32)
    ident_bf = sb("ident_bf", [128, 128], BF16)
    ident_f = sb("ident_f", [128, 128], F32)
    mtri = sb("mtri", [128, 128], F32)
    b1_fm = sb("b1_fm", [128, NE * HC], F32)
    rb1_fm = sb("rb1_fm", [128, NE * HC], F32)
    rb1_bf = sb("rb1_bf", [128, NE * HC], BF16)
    m_all = sb("m_all", [128, NE * NTC], F32)
    b2_bf = sb("b2_bf", [8, E], BF16)
    wr_sb = sb("wr_sb", [128, EC * NE], F32)
    wr_hi = sb("wr_hi", [128, EC * NE], BF16)
    wr_lo = sb("wr_lo", [128, EC * NE], BF16)
    br_col = sb("br_col", [8, 1], F32)
    ones1 = sb("ones1", [1, 128], BF16)
    ones8 = sb("ones8", [8, 1], BF16)
    csb = sb("csb", [1, E], BF16)
    cbc = sb("cbc", [128, E], F32)
    lg_sb = sb("lg_sb", [8, 128], F32)
    lgt = sb("lgt", [128, 8], F32)
    lnc = sb("lncols", [128, 8], F32)
    wes = ExitStack()
    w1sb = wes.enter_context(nc.sbuf_tensor("w1sb", [128, EC * HID], BF16))
    w2sb = wes.enter_context(nc.sbuf_tensor("w2sb", [128, HC * E], BF16))

    def sem(name):
        return Ctr(es.enter_context(nc.semaphore(name)))

    s_pre = sem("s_pre")
    g_pre = sem("g_pre")
    s_xin = [sem("s_xin0"), sem("s_xin1")]
    s_xsw = [sem("s_xsw0"), sem("s_xsw1")]
    s_xg = [sem("s_xg0"), sem("s_xg1")]
    g_w1 = sem("g_w1")
    g_w2 = sem("g_w2")
    g_z = [sem("g_z0"), sem("g_z1")]
    g_wf = sem("g_wf")
    g_ab = sem("g_ab")
    g_bb = sem("g_bb")
    s_xn3 = sem("s_xn3")
    s_zi = [sem("s_zi0"), sem("s_zi1")]
    s_xe = [sem("s_xe0"), sem("s_xe1")]
    s_out = [sem("s_out0"), sem("s_out1")]
    pe = sem("pe")
    ve = sem("ve")
    ae = sem("ae")
    ge = sem("ge")

    block = es.enter_context(nc.Block())
    p1es = ExitStack()
    b1_sb = p1es.enter_context(nc.sbuf_tensor("b1_sb", [8, HID], F32))

    with es:
        @block.sync
        def _(sy):
            sy.dma_start(b1_sb[:, :], b1_d[:, :]).then_inc(s_pre.h, 16)
            sy.dma_start(br_col[:, :], br_d[:, :]).then_inc(s_pre.h, 16)
            sy.dma_start(wr_sb[:, :],
                         bass.AP(wr_d, 0, [[NE, 128], [128 * NE, EC], [1, NE]])
                         ).then_inc(s_pre.h, 16)
            s_pre.n = 48

        @block.gpsimd
        def _(g):
            cg = Chain(g, ge)
            cg.iota(ident_i[:, :], pattern=[[-1, 128]], base=0,
                    channel_multiplier=1, allow_small_or_imprecise_dtypes=True)
            cg.memset(ones1[:, :], 1.0)
            cg.memset(ones8[:, :], 1.0)
            g.dma_start(b2_bf[:, :], b2_d[:, :]).then_inc(g_pre.h, 16)
            g_pre.n += 16
            for d in range(EC):
                g.dma_start(w1sb[:, d * HID:(d + 1) * HID],
                            w1_d[d * 128:(d + 1) * 128, :]).then_inc(g_w1.h, 16)
            g_w1.n += 16 * EC
            for h0 in range(0, HC, W2G):
                g.dma_start(
                    bass.AP(w2sb, h0 * E, [[HC * E, 128], [E, W2G], [1, E]]),
                    bass.AP(w2_d, h0 * 128 * E, [[E, 128], [128 * E, W2G], [1, E]]),
                ).then_inc(g_w2.h, 16)
            g_w2.n += 16 * (HC // W2G)

        @block.vector
        def _(v):
            cv = Chain(v, ve)
            ge.wait(v)
            cv.tensor_scalar(ident_bf[:, :], ident_i[:, :], 0.0, None,
                             ALU.is_equal)
            cv.tensor_scalar(ident_f[:, :], ident_i[:, :], 0.0, None,
                             ALU.is_equal)
            cv.tensor_scalar(mtri[:, :], ident_i[:, :], 0.0, NEG, ALU.is_lt,
                             ALU.mult)
            s_pre.wait(v)
            cv.tensor_copy(wr_hi[:, :], wr_sb[:, :])
            cv.tensor_tensor(wr_lo[:, :], wr_sb[:, :], wr_hi[:, :], ALU.subtract)

        with nc.psum_tensor("ptr8", [128, 8], F32) as ptr8:
            for c in range(HC):
                @block.tensor
                def _(t, c=c):
                    ve.wait(t)
                    pe.inc(t.transpose(ptr8[:, :],
                                       b1_sb[0:8, c * 128:(c + 1) * 128],
                                       ident_f[0:8, 0:8]))

                @block.vector
                def _(v, c=c):
                    cv = Chain(v, ve)
                    pe.wait(v)
                    cv.tensor_copy(
                        bass.AP(b1_fm, c, [[NE * HC, 128], [HC, 8]]),
                        ptr8[:, :])

            @block.vector
            def _(v):
                cv = Chain(v, ve)
                cv.tensor_scalar(rb1_fm[:, :], b1_fm[:, :], 0.0, None, ALU.max)
                cv.tensor_copy(rb1_bf[:, :], rb1_fm[:, :])

        def sb1(name, shape, dt):
            return p1es.enter_context(nc.sbuf_tensor(name, shape, dt))

        xbuf = [sb1("xbuf0", [128, E], F32), sb1("xbuf1", [128, E], F32)]
        xnf = [sb1("xnf0", [128, E], F32), sb1("xnf1", [128, E], F32)]
        xnh = sb1("xnh", [128, E], BF16)
        xnl = sb1("xnl", [128, E], BF16)
        stg = [sb1("stg0", [128, E], BF16), sb1("stg1", [128, E], BF16)]
        stl = sb1("stl", [128, E], BF16)

        p1_sqrt = {}
        p1_hilo = {}
        p1_tr = {}
        p1_lg = {}
        p1_stage_d = {}
        IVE = 1.0 / E
        with nc.psum_tensor("pst0", [128, 128], BF16) as pst0, \
             nc.psum_tensor("pst1", [128, 128], BF16) as pst1, \
             nc.psum_tensor("lgps", [8, 128], F32) as lgps, \
             nc.psum_tensor("lgtp", [128, 8], F32) as lgtp:
            pst = [pst0, pst1]
            for j in range(min(2, NTC)):
                @block.sync
                def _(sy, j=j):
                    sy.dma_start(xbuf[j % 2][:, :],
                                 x_d[j * 128:(j + 1) * 128, :]
                                 ).then_inc(s_xin[j % 2].h, 16)
                    s_xin[j % 2].n += 16

            for j in range(NTC):
                bi = j % 2

                @block.vector
                def _(v, j=j, bi=bi):
                    cv = Chain(v, ve)
                    s_xin[bi].wait(v)
                    if j >= 1:
                        ae.wait(v, p1_sqrt[j - 1])
                    xt = xbuf[bi]
                    cv.scalar_tensor_tensor(xnf[bi][:, :], xt[:, :], 1.0,
                                            xt[:, :], ALU.mult, ALU.mult,
                                            accum_out=lnc[:, 1:2])
                    cv.reduce_sum(lnc[:, 0:1], xt[:, :], axis=AXL.X)
                    cv.tensor_scalar(lnc[:, 2:3], lnc[:, 0:1], IVE, None,
                                     ALU.mult)
                    cv.tensor_scalar(lnc[:, 3:4], lnc[:, 1:2], IVE, None,
                                     ALU.mult)
                    cv.scalar_tensor_tensor(lnc[:, 4:5], lnc[:, 2:3],
                                            lnc[:, 2:3], lnc[:, 3:4],
                                            ALU.mult, ALU.subtract)
                    cv.tensor_scalar(lnc[:, 5:6], lnc[:, 4:5], -1.0, 1e-5,
                                     ALU.mult, ALU.add)

                @block.scalar
                def _(s, j=j):
                    ca = Chain(s, ae)
                    ve.wait(s)
                    ca.activation(lnc[:, 6:7], lnc[:, 5:6], AF.Sqrt)
                    p1_sqrt[j] = ae.n

                @block.vector
                def _(v, j=j, bi=bi):
                    cv = Chain(v, ve)
                    ae.wait(v, p1_sqrt[j])
                    if j >= 1:
                        pe.wait(v, p1_tr[j - 1])
                    cv.reciprocal(lnc[:, 7:8], lnc[:, 6:7])
                    cv.scalar_tensor_tensor(lnc[:, 4:5], lnc[:, 2:3], -1.0,
                                            lnc[:, 7:8], ALU.mult, ALU.mult)
                    cv.tensor_scalar(xnf[bi][:, :], xbuf[bi][:, :],
                                     lnc[:, 7:8], lnc[:, 4:5], ALU.mult,
                                     ALU.add)
                    cv.tensor_copy(xnh[:, :], xnf[bi][:, :])
                    cv.tensor_tensor(xnl[:, :], xnf[bi][:, :], xnh[:, :],
                                     ALU.subtract)
                    p1_hilo[j] = ve.n

                @block.sync
                def _(sy, j=j):
                    if j + 2 < NTC:
                        sy.wait_ge(ve.h, p1_hilo[j])
                        sy.dma_start(xbuf[j % 2][:, :],
                                     x_d[(j + 2) * 128:(j + 3) * 128, :]
                                     ).then_inc(s_xin[j % 2].h, 16)
                        s_xin[j % 2].n += 16

                @block.tensor
                def _(t, j=j):
                    ve.wait(t, p1_hilo[j])
                    if j >= 2:
                        s_xsw[j % 2].wait(t)
                    ab = ae.n
                    for c in range(EC):
                        w = ab + c - 1
                        if w >= 1:
                            t.wait_ge(ae.h, w)
                        pe.inc(t.transpose(pst[c % 2][:, :],
                                           xnh[:, c * 128:(c + 1) * 128],
                                           ident_bf[:, :]))
                    for c in range(EC):
                        if c < 2:
                            c2 = EC - 1 if (EC - 1) % 2 == c % 2 else EC - 2
                            w = ab + c2 + 1 if c2 >= 0 else ab
                        else:
                            w = ab + EC + c - 1
                        if w >= 1:
                            t.wait_ge(ae.h, w)
                        pe.inc(t.transpose(pst[c % 2][:, :],
                                           xnl[:, c * 128:(c + 1) * 128],
                                           ident_bf[:, :]))
                    p1_tr[j] = pe.n

                @block.scalar
                def _(s, j=j):
                    ca = Chain(s, ae)
                    base = pe.n - 2 * EC
                    for c in range(EC):
                        s.wait_ge(pe.h, base + c + 1)
                        ca.activation(stg[j % 2][:, c * 128:(c + 1) * 128],
                                      pst[c % 2][:, :], AF.Copy)
                    for c in range(EC):
                        s.wait_ge(pe.h, base + EC + c + 1)
                        ca.activation(stl[:, c * 128:(c + 1) * 128],
                                      pst[c % 2][:, :], AF.Copy)
                    p1_stage_d[j] = ae.n

                @block.tensor
                def _(t, j=j):
                    ae.wait(t, p1_stage_d[j])
                    if j >= 1:
                        ve.wait(t, p1_lg[j - 1])
                    first = True
                    for c in range(EC):
                        t.matmul(lgps[:, :], wr_hi[:, c * NE:(c + 1) * NE],
                                 stg[j % 2][:, c * 128:(c + 1) * 128],
                                 start=first, stop=False)
                        first = False
                        t.matmul(lgps[:, :], wr_lo[:, c * NE:(c + 1) * NE],
                                 stg[j % 2][:, c * 128:(c + 1) * 128],
                                 start=False, stop=False)
                        mm = t.matmul(lgps[:, :], wr_hi[:, c * NE:(c + 1) * NE],
                                      stl[:, c * 128:(c + 1) * 128],
                                      start=False, stop=(c == EC - 1))
                    pe.inc(mm)

                @block.vector
                def _(v, j=j):
                    cv = Chain(v, ve)
                    pe.wait(v)
                    cv.tensor_scalar(lg_sb[:, :], lgps[:, :], br_col[:, 0:1],
                                     None, ALU.add)
                    p1_lg[j] = ve.n

                @block.tensor
                def _(t, j=j):
                    ve.wait(t, p1_lg[j])
                    pe.inc(t.transpose(lgtp[:, :], lg_sb[:, :],
                                       ident_f[0:8, 0:8]))

                @block.vector
                def _(v, j=j):
                    cv = Chain(v, ve)
                    pe.wait(v)
                    cv.tensor_copy(lgt[:, :], lgtp[:, :])
                    cv.reduce_max(lnc[:, 0:1], lgt[:, :], axis=AXL.X)
                    cv.tensor_scalar(
                        bass.AP(m_all, j, [[NE * NTC, 128], [NTC, NE]]),
                        lgt[:, :], lnc[:, 0:1], None, ALU.is_ge)

                @block.sync
                def _(sy, j=j):
                    sy.wait_ge(ae.h, p1_stage_d[j])
                    sy.dma_start(
                        bass.AP(xs_d, j * 128,
                                [[NT, 128], [128 * NT, EC], [1, 128]]),
                        bass.AP(stg[j % 2], 0, [[E, 128], [128, EC], [1, 128]]),
                    ).then_inc(s_xsw[j % 2].h, 16)
                    s_xsw[j % 2].n += 16

        p1es.close()
        p2es = ExitStack()

        def sb2(name, shape, dt):
            return p2es.enter_context(nc.sbuf_tensor(name, shape, dt))

        xgb = [sb2("xgb0", [128, EC * TBL], BF16), sb2("xgb1", [128, EC * TBL], BF16)]
        h1p = sb2("h1p", [128, HC * TBL], BF16)
        scb = [sb2("scb0", [128, TBL], F32), sb2("scb1", [128, TBL], F32)]
        msc = [sb2("msc0", [128, E], BF16), sb2("msc1", [128, E], BF16)]

        p2_peA = {}
        p2_peB = {}
        p2_veh1 = {}
        p2_peC = {}
        msk_counts = []

        with nc.psum_tensor("h1ps0", [128, TBL], F32) as h1ps0, \
             nc.psum_tensor("h1ps1", [128, TBL], F32) as h1ps1, \
             nc.psum_tensor("zps0", [128, E], F32) as zps0, \
             nc.psum_tensor("zps1", [128, E], F32) as zps1, \
             nc.psum_tensor("cps", [1, E], F32) as cps:
            h1ps = [h1ps0, h1ps1]
            zps = [zps0, zps1]

            for gi0 in range(min(2, NE * NTB)):
                @block.sync
                def _(sy, gi0=gi0):
                    tb = gi0 % NTB
                    s_xsw[0].wait(sy)
                    s_xsw[1].wait(sy)
                    for c in range(EC):
                        sy.dma_start(
                            xgb[gi0 % 2][:, c * TBL:(c + 1) * TBL],
                            xs_d[c * 128:(c + 1) * 128, tb * TBL:(tb + 1) * TBL],
                        ).then_inc(s_xg[gi0 % 2].h, 16)
                    s_xg[gi0 % 2].n += 16 * EC

            for e in range(NE):
                for tb in range(NTB):
                    gi = e * NTB + tb

                    @block.tensor
                    def _(t, e=e, tb=tb, gi=gi):
                        s_xg[gi % 2].wait(t)
                        g_w1.wait(t, 16 * EC * (e + 1))
                        if gi == 0:
                            ve.wait(t)
                        ab = ae.n
                        for h in range(HC):
                            w = ab + h - 1
                            if w >= 1:
                                t.wait_ge(ae.h, w)
                            for d in range(EC):
                                mm = t.matmul(
                                    h1ps[h % 2][:, 0:TBL],
                                    w1sb[:, d * HID + h * 128:
                                         d * HID + (h + 1) * 128],
                                    xgb[gi % 2][:, d * TBL:(d + 1) * TBL],
                                    start=(d == 0), stop=(d == EC - 1))
                            pe.inc(mm)
                        p2_peA[(e, tb)] = pe.n

                    @block.scalar
                    def _(s, e=e, tb=tb):
                        ca = Chain(s, ae)
                        pb = p2_peA[(e, tb)] - HC
                        vb = ve.n
                        for h in range(HC):
                            s.wait_ge(pe.h, pb + h + 1)
                            w = vb + h - 1
                            if w >= 1:
                                s.wait_ge(ve.h, w)
                            ca.activation(
                                scb[h % 2][:, :], h1ps[h % 2][:, 0:TBL],
                                AF.Relu,
                                bias=b1_fm[:, e * HC + h:e * HC + h + 1])

                    @block.vector
                    def _(v, e=e, tb=tb, gi=gi):
                        cv = Chain(v, ve)
                        ab = ae.n - HC
                        if gi > 0:
                            v.wait_ge(pe.h, list(p2_peB.values())[-1])
                        for h in range(HC):
                            v.wait_ge(ae.h, ab + h + 1)
                            cv.tensor_scalar(
                                h1p[:, h * TBL:(h + 1) * TBL],
                                scb[h % 2][:, :],
                                rb1_fm[:, e * HC + h:e * HC + h + 1], None,
                                ALU.subtract)
                        p2_veh1[(e, tb)] = ve.n

                    @block.sync
                    def _(sy, e=e, tb=tb, gi=gi):
                        gi2 = gi + 2
                        if gi2 < NE * NTB:
                            tb2 = gi2 % NTB
                            sy.wait_ge(pe.h, p2_peA[(e, tb)])
                            for c in range(EC):
                                sy.dma_start(
                                    xgb[gi2 % 2][:, c * TBL:(c + 1) * TBL],
                                    xs_d[c * 128:(c + 1) * 128,
                                         tb2 * TBL:(tb2 + 1) * TBL],
                                ).then_inc(s_xg[gi2 % 2].h, 16)
                            s_xg[gi2 % 2].n += 16 * EC

                    @block.tensor
                    def _(t, e=e, tb=tb, gi=gi):
                        ve.wait(t, p2_veh1[(e, tb)])
                        g_w2.wait(t, 16 * (HC // W2G) * (e + 1))
                        for tc in range(TCB):
                            gtc = gi * TCB + tc
                            prev = gtc - 2
                            if prev >= 0:
                                if prev >= gi * TCB:
                                    t.wait_ge(ve.h,
                                              p2_veh1[(e, tb)] + (prev - gi * TCB) + 1)
                                else:
                                    t.wait_ge(ve.h, msk_counts[prev])
                            for h in range(HC):
                                for half in range(EH):
                                    mm = t.matmul(
                                        zps[tc % 2][:, half * EW:(half + 1) * EW],
                                        h1p[:, h * TBL + tc * 128:
                                             h * TBL + tc * 128 + 128],
                                        w2sb[:, h * E + half * EW:
                                             h * E + (half + 1) * EW],
                                        start=(h == 0), stop=(h == HC - 1))
                            pe.inc(mm)
                            p2_peB[(e, tb, tc)] = pe.n

                    @block.vector
                    def _(v, e=e, tb=tb, gi=gi):
                        cv = Chain(v, ve)
                        for tc in range(TCB):
                            gtc = gi * TCB + tc
                            jt = tb * TCB + tc
                            v.wait_ge(pe.h, p2_peB[(e, tb, tc)])
                            if gtc >= 2:
                                v.wait_ge(g_z[gtc % 2].h, 16 * (gtc // 2))
                            cv.tensor_scalar(
                                msc[tc % 2][:, :], zps[tc % 2][:, :],
                                m_all[:, e * NTC + jt:e * NTC + jt + 1], None,
                                ALU.mult)
                            msk_counts.append(ve.n)

                    @block.gpsimd
                    def _(g, e=e, tb=tb, gi=gi):
                        for tc in range(TCB):
                            gtc = gi * TCB + tc
                            jt = tb * TCB + tc
                            g.wait_ge(ve.h, msk_counts[gtc])
                            if e > 0 and tb == 0 and tc == 0:
                                g_z[0].wait(g)
                                g_z[1].wait(g)
                            kw = {} if e == 0 else dict(accum_op=ALU.add)
                            g.dma_start(z_d[jt * 128:(jt + 1) * 128, :],
                                        msc[tc % 2][:, :], **kw
                                        ).then_inc(g_z[tc % 2].h, 16)
                            g_z[tc % 2].n += 16

                @block.tensor
                def _(t, e=e):
                    for h in range(HC):
                        for half in range(EH):
                            mm = t.matmul(
                                cps[0:1, half * EW:(half + 1) * EW],
                                rb1_bf[:, e * HC + h:e * HC + h + 1],
                                w2sb[:, h * E + half * EW:
                                     h * E + (half + 1) * EW],
                                start=(e == 0 and h == 0), stop=False,
                                skip_group_check=True)
                    pe.inc(mm)
                    p2_peC[e] = pe.n

                @block.gpsimd
                def _(g, e=e):
                    if e + 1 < NE:
                        g.wait_ge(pe.h, p2_peA[(e, NTB - 1)])
                        for d in range(EC):
                            g.dma_start(
                                w1sb[:, d * HID:(d + 1) * HID],
                                w1_d[(e + 1) * E + d * 128:
                                     (e + 1) * E + (d + 1) * 128, :]
                            ).then_inc(g_w1.h, 16)
                        g_w1.n += 16 * EC
                        g.wait_ge(pe.h, p2_peC[e])
                        for h0 in range(0, HC, W2G):
                            g.dma_start(
                                bass.AP(w2sb, h0 * E,
                                        [[HC * E, 128], [E, W2G], [1, E]]),
                                bass.AP(w2_d, ((e + 1) * HID + h0 * 128) * E,
                                        [[E, 128], [128 * E, W2G], [1, E]]),
                            ).then_inc(g_w2.h, 16)
                        g_w2.n += 16 * (HC // W2G)

            @block.tensor
            def _(t):
                g_pre.wait(t, 16)
                for half in range(EH):
                    mm = t.matmul(cps[0:1, half * EW:(half + 1) * EW],
                                  ones8[:, 0:1],
                                  b2_bf[:, half * EW:(half + 1) * EW],
                                  start=False, stop=True,
                                  skip_group_check=True)
                pe.inc(mm)

            @block.vector
            def _(v):
                cv = Chain(v, ve)
                pe.wait(v)
                cv.tensor_copy(csb[:, :], cps[0:1, :])

        with nc.psum_tensor("cbps", [128, E], F32) as cbps:
            @block.tensor
            def _(t):
                ve.wait(t)
                for half in range(EH):
                    mm = t.matmul(cbps[:, half * EW:(half + 1) * EW],
                                  ones1[:, :], csb[0:1, half * EW:(half + 1) * EW],
                                  start=True, stop=True)
                pe.inc(mm)

            @block.vector
            def _(v):
                cv = Chain(v, ve)
                pe.wait(v)
                cv.tensor_copy(cbc[:, :], cbps[:, :])

        if debug:
            @block.sync
            def _(sy):
                g_z[0].wait(sy)
                g_z[1].wait(sy)
                sy.dma_start(dz_d[:, :], z_d[:, :]).then_inc(s_pre.h, 16)
                sy.dma_start(dxs_d[:, :], xs_d[:, :]).then_inc(s_pre.h, 16)
                ve.wait(sy)
                sy.dma_start(dm_d[:, :], m_all[:, :]).then_inc(s_pre.h, 16)
                sy.dma_start(dc_d[:, :], cbc[:, :]).then_inc(s_pre.h, 16)
                s_pre.n += 64

        p2es.close()
        wes.close()
        jbc = sb("jbc", [128, T], F32)
        k_fm = sb("k_fm", [128, EC * T], BF16)
        v_tm = sb("v_tm", [128, (T // 128) * E], BF16)
        q_fm = sb("q_fm", [128, EC * T], BF16)
        o_fm = sb("o_fm", [128, EC * T], BF16)
        wbuf = sb("wbuf", [128, EC * E], BF16)
        ab_sb = sb("ab_sb", [8, E], BF16)
        bb_sb = sb("bb_sb", [128, EC * R], BF16)
        u_sb = sb("u_sb", [8, T], BF16)

        p3_pe = {}
        p3_pe[(-1, 'end')] = pe.n
        p3_uve = {}
        p3_sv = {}
        prev_sv_last = [None]

        @block.gpsimd
        def _(g):
            cg = Chain(g, ge)
            g.wait_ge(pe.h, p3_pe[(-1, 'end')])
            cg.iota(jbc[:, :], pattern=[[1, T]], base=0, channel_multiplier=0,
                    allow_small_or_imprecise_dtypes=True)
        ge_jbc = ge.n

        for sq in range(NSEQ):
            t0 = sq * T
            pAes = ExitStack()
            xn3 = pAes.enter_context(
                nc.sbuf_tensor("xn3_" + str(sq), [128, EC * T], BF16))

            @block.sync
            def _(sy, sq=sq, t0=t0, xn3=xn3):
                if sq > 0:
                    pe.wait(sy)
                    sy.wait_ge(ve.h, prev_sv_last[0])
                    s_out[0].wait(sy)
                    s_out[1].wait(sy)
                else:
                    s_xsw[0].wait(sy)
                    s_xsw[1].wait(sy)
                    sy.wait_ge(pe.h, p3_pe[(-1, 'end')])
                for c in range(EC):
                    sy.dma_start(xn3[:, c * T:(c + 1) * T],
                                 xs_d[c * 128:(c + 1) * 128, t0:t0 + T]
                                 ).then_inc(s_xn3.h, 16)
                s_xn3.n += 16 * EC

            def emit_u(bd, sq, xn3, first):
                @block.gpsimd
                def _(g):
                    w = p3_pe.get((sq, 'wfree'),
                                  p3_pe.get((sq - 1, 'end'), 0))
                    if w:
                        g.wait_ge(pe.h, w)
                    g.dma_start(
                        bb_sb[:, :],
                        bass.AP(bd, 0, [[R, 128], [128 * R, EC], [1, R]]),
                    ).then_inc(g_bb.h, 16)
                    g_bb.n += 16

                with nc.psum_tensor("ups_u" + str(sq) + str(g_bb.n),
                                    [8, T], F32) as ups:
                    @block.tensor
                    def _(t):
                        g_bb.wait(t)
                        s_xn3.wait(t)
                        ae.wait(t)
                        if first:
                            ve.wait(t)
                        if sq > 0 and prev_sv_last[0] is not None:
                            t.wait_ge(ve.h, prev_sv_last[0])
                        w = p3_pe.get((sq, 'wfree'), 0)
                        if w:
                            t.wait_ge(ve.h, p3_uve[(sq, 'last')])
                        for c in range(EC):
                            for nb in range(NPB):
                                mm = t.matmul(
                                    ups[:, nb * PB:(nb + 1) * PB],
                                    bb_sb[:, c * R:(c + 1) * R],
                                    xn3[:, c * T + nb * PB:
                                        c * T + (nb + 1) * PB],
                                    start=(c == 0), stop=(c == EC - 1))
                        pe.inc(mm)

                    @block.vector
                    def _(v):
                        cv = Chain(v, ve)
                        pe.wait(v)
                        cv.tensor_scalar(u_sb[:, :], ups[:, :], LORA_S, None,
                                         ALU.mult)
                        p3_uve[(sq, 'last')] = ve.n

            def proj_fm(wT_d, aT_d, bT_d, dest, scale, sq, tag, xn3,
                        first=False):
                emit_u(bT_d, sq, xn3, first)

                @block.gpsimd
                def _(g):
                    w = p3_pe.get((sq, 'wfree'),
                                  p3_pe.get((sq - 1, 'end'), 0))
                    if w:
                        g.wait_ge(pe.h, w)
                    for c in range(EC):
                        g.dma_start(wbuf[:, c * E:(c + 1) * E],
                                    wT_d[c * 128:(c + 1) * 128, :]
                                    ).then_inc(g_wf.h, 16)
                    g_wf.n += 16 * EC
                    g.dma_start(ab_sb[:, :], aT_d[:, :]).then_inc(g_ab.h, 16)
                    g_ab.n += 16

                with nc.psum_tensor("psA_" + tag + str(sq), [128, 4 * 512],
                                    F32) as psA:
                    for ic in range(EC):
                        @block.tensor
                        def _(t, ic=ic):
                            g_wf.wait(t)
                            g_ab.wait(t)
                            ve.wait(t, p3_uve[(sq, 'last')])
                            ae.wait(t)
                            for ec in range(EC):
                                for nb in range(NPB):
                                    t.matmul(
                                        psA[:, (nb % 4) * 512:
                                            (nb % 4) * 512 + PB],
                                        wbuf[:, ec * E + ic * 128:
                                             ec * E + ic * 128 + 128],
                                        xn3[:, ec * T + nb * PB:
                                            ec * T + (nb + 1) * PB],
                                        start=(ec == 0), stop=False)
                            for nb in range(NPB):
                                mm = t.matmul(
                                    psA[:, (nb % 4) * 512:(nb % 4) * 512 + PB],
                                    ab_sb[0:8, ic * 128:(ic + 1) * 128],
                                    u_sb[0:8, nb * PB:(nb + 1) * PB],
                                    start=False, stop=True)
                            pe.inc(mm)

                        @block.scalar
                        def _(s, ic=ic):
                            ca = Chain(s, ae)
                            pe.wait(s)
                            for nb in range(NPB):
                                ca.activation(
                                    dest[:, ic * T + nb * PB:
                                         ic * T + (nb + 1) * PB],
                                    psA[:, (nb % 4) * 512:(nb % 4) * 512 + PB],
                                    AF.Copy, scale=scale)
                p3_pe[(sq, 'wfree')] = pe.n

            proj_fm(wkT_d, akT_d, bkT_d, k_fm, 1.0, sq, "k", xn3,
                    first=(sq == 0))

            emit_u(bvT_d, sq, xn3, False)

            @block.gpsimd
            def _(g, sq=sq):
                g.wait_ge(pe.h, p3_pe[(sq, 'wfree')])
                for c in range(EC):
                    g.dma_start(wbuf[:, c * E:(c + 1) * E],
                                wvT_d[c * 128:(c + 1) * 128, :]
                                ).then_inc(g_wf.h, 16)
                g_wf.n += 16 * EC
                g.dma_start(ab_sb[:, :], avT_d[:, :]).then_inc(g_ab.h, 16)
                g_ab.n += 16

            with nc.psum_tensor("psV" + str(sq), [128, E], F32) as psV:
                for tc in range(T // 128):
                    @block.tensor
                    def _(t, tc=tc, sq=sq):
                        g_wf.wait(t)
                        g_ab.wait(t)
                        ve.wait(t, p3_uve[(sq, 'last')])
                        ae.wait(t)
                        for ec in range(EC):
                            for half in range(EH):
                                t.matmul(
                                    psV[:, half * EW:(half + 1) * EW],
                                    xn3[:, ec * T + tc * 128:
                                        ec * T + tc * 128 + 128],
                                    wbuf[:, ec * E + half * EW:
                                         ec * E + (half + 1) * EW],
                                    start=(ec == 0), stop=False)
                        for half in range(EH):
                            mm = t.matmul(
                                psV[:, half * EW:(half + 1) * EW],
                                u_sb[0:8, tc * 128: tc * 128 + 128],
                                ab_sb[0:8, half * EW:(half + 1) * EW],
                                start=False, stop=True)
                        pe.inc(mm)

                    @block.scalar
                    def _(s, tc=tc):
                        ca = Chain(s, ae)
                        pe.wait(s)
                        ca.activation(v_tm[:, tc * E:(tc + 1) * E],
                                      psV[:, :], AF.Copy)
            p3_pe[(sq, 'wfree')] = pe.n

            proj_fm(wqT_d, aqT_d, bqT_d, q_fm, SCL, sq, "q", xn3)
            proj_ae_done = ae.n

            @block.gpsimd
            def _(g, sq=sq):
                g.wait_ge(pe.h, p3_pe[(sq, 'wfree')])
                for c in range(EC):
                    g.dma_start(wbuf[:, c * E:(c + 1) * E],
                                wpT_d[c * 128:(c + 1) * 128, :]
                                ).then_inc(g_wf.h, 16)
                g_wf.n += 16 * EC

            pAes.close()
            pBes = ExitStack()

            def sb3(name, shape, dt):
                return pBes.enter_context(
                    nc.sbuf_tensor(name + "_" + str(sq), shape, dt))

            colfac = sb3("colfac", [128, T], BF16)
            praw = sb3("praw", [128, T], BF16)
            pnrm = praw
            pt_sb = sb3("pt_sb", [128, T], BF16)
            ztb = [sb3("ztb0", [128, E], BF16), sb3("ztb1", [128, E], BF16)]
            xtb = [sb3("xtb0", [128, E], F32), sb3("xtb1", [128, E], F32)]
            gts = sb3("gts", [128, E], BF16)
            otb = [sb3("otb0", [128, E], F32), sb3("otb1", [128, E], F32)]

            with nc.psum_tensor("sbig" + str(sq), [128, SW], F32) as sbig, \
                 nc.psum_tensor("ptp0" + str(sq), [128, 128], BF16) as ptp0, \
                 nc.psum_tensor("ptp1" + str(sq), [128, 128], BF16) as ptp1, \
                 nc.psum_tensor("ops0" + str(sq), [64, 128], F32) as ops0, \
                 nc.psum_tensor("ops1" + str(sq), [64, 128], F32) as ops1:
                ptp = [ptp0, ptp1]
                ops = [ops0, ops1]
                it = 0
                p3_exp = {}
                p3_oev = {}
                p3_pnm = {}
                last_user = {0: None, 1: None}
                for h in range(H):
                    hk = h // 2
                    hb = (h % 2) * 64

                    @block.scalar
                    def _(s, h=h):
                        ca = Chain(s, ae)
                        ge.wait(s, ge_jbc)
                        if h >= 1:
                            ve.wait(s)
                        ca.activation(colfac[:, :], jbc[:, :], AF.Exp,
                                      scale=-slopes[h])

                    for qt in range(QTN):
                        q0 = qt * 128
                        W = (qt + 1) * 128
                        covS = min(((q0 + 128 + 511) // 512) * 512, T)
                        small = covS <= SW // 2
                        soff = (SW // 2) * (it % 2) if small else 0
                        regions = [it % 2] if small else [0, 1]
                        deps = [last_user[r] for r in regions
                                if last_user[r] is not None]

                        @block.tensor
                        def _(t, it=it, soff=soff, covS=covS, hk=hk, hb=hb,
                              q0=q0, deps=tuple(deps), sq=sq,
                              proj_ae_done=proj_ae_done):
                            if deps:
                                t.wait_ge(ae.h, p3_exp[max(deps)])
                            elif sq > 0 and prev_sv_last[0] is not None:
                                t.wait_ge(ve.h, prev_sv_last[0])
                            if it == 0:
                                t.wait_ge(ae.h, proj_ae_done)
                            nkc = (covS + 511) // 512
                            for nk in range(nkc):
                                n0 = nk * 512
                                n1 = min(n0 + 512, covS)
                                mm = t.matmul(
                                    sbig[:, soff + n0: soff + n1],
                                    q_fm[hb:hb + 64,
                                         hk * T + q0: hk * T + q0 + 128],
                                    k_fm[hb:hb + 64, hk * T + n0: hk * T + n1],
                                    start=True, stop=True)
                            pe.inc(mm)

                        for r in regions:
                            last_user[r] = it

                        @block.vector
                        def _(v, it=it, soff=soff, q0=q0, W=W):
                            cv = Chain(v, ve)
                            pe.wait(v)
                            if it >= 1:
                                ae.wait(v, p3_exp[it - 1])
                            cv.tensor_tensor(
                                sbig[:, soff + q0: soff + q0 + 128],
                                mtri[:, :],
                                sbig[:, soff + q0: soff + q0 + 128],
                                ALU.add)
                            cv.reduce_max(lnc[:, 0:1], sbig[:, soff:soff + W],
                                          axis=AXL.X, negate=True)

                        @block.scalar
                        def _(s, it=it, soff=soff, W=W):
                            ca = Chain(s, ae)
                            ve.wait(s)
                            ca.activation(praw[:, 0:W], sbig[:, soff:soff + W],
                                          AF.Exp, bias=lnc[:, 0:1])
                            p3_exp[it] = ae.n

                        @block.vector
                        def _(v, it=it, W=W):
                            cv = Chain(v, ve)
                            ae.wait(v, p3_exp[it])
                            cv.scalar_tensor_tensor(pnrm[:, 0:W], praw[:, 0:W],
                                                    1.0, colfac[:, 0:W],
                                                    ALU.mult, ALU.mult,
                                                    accum_out=lnc[:, 1:2])
                            cv.reciprocal(lnc[:, 2:3], lnc[:, 1:2])
                            cv.tensor_scalar(pnrm[:, 0:W], pnrm[:, 0:W],
                                             lnc[:, 2:3], None, ALU.mult)
                            p3_pnm[it] = ve.n

                        @block.tensor
                        def _(t, it=it, qt=qt):
                            ve.wait(t, p3_pnm[it])
                            vb = ve.n
                            for kc in range(qt + 1):
                                w = vb + kc - 1
                                if w > p3_pnm[it]:
                                    t.wait_ge(ve.h, w)
                                pe.inc(t.transpose(
                                    ptp[kc % 2][:, :],
                                    pnrm[:, kc * 128:(kc + 1) * 128],
                                    ident_bf[:, :]))

                        @block.vector
                        def _(v, it=it, qt=qt):
                            cv = Chain(v, ve)
                            pb = pe.n - (qt + 1)
                            for kc in range(qt + 1):
                                v.wait_ge(pe.h, pb + kc + 1)
                                cv.tensor_copy(
                                    pt_sb[:, kc * 128:(kc + 1) * 128],
                                    ptp[kc % 2][:, :])

                        @block.tensor
                        def _(t, it=it, qt=qt, h=h):
                            vb = ve.n - (qt + 1)
                            if it >= 2:
                                t.wait_ge(ve.h, p3_oev[it - 2])
                            for kc in range(qt + 1):
                                t.wait_ge(ve.h, vb + kc + 1)
                                mm = t.matmul(
                                    ops[it % 2][:, :],
                                    v_tm[:, kc * E + h * DH:
                                         kc * E + (h + 1) * DH],
                                    pt_sb[:, kc * 128:(kc + 1) * 128],
                                    start=(kc == 0), stop=(kc == qt))
                            pe.inc(mm)

                        @block.vector
                        def _(v, it=it, qt=qt, hk=hk, hb=hb):
                            cv = Chain(v, ve)
                            pe.wait(v)
                            cv.tensor_copy(
                                o_fm[hb:hb + 64,
                                     hk * T + qt * 128: hk * T + qt * 128 + 128],
                                ops[it % 2][:, :])
                            p3_oev[it] = ve.n

                        it += 1

                for gi0 in range(min(2, QTN)):
                    @block.sync
                    def _(sy, gi0=gi0, sq=sq, t0=t0):
                        g_z[0].wait(sy)
                        g_z[1].wait(sy)
                        pe.wait(sy)
                        sy.dma_start(
                            ztb[gi0 % 2][:, :],
                            z_d[t0 + gi0 * 128: t0 + (gi0 + 1) * 128, :]
                        ).then_inc(s_zi[gi0 % 2].h, 16)
                        s_zi[gi0 % 2].n += 16
                        sy.dma_start(
                            xtb[gi0 % 2][:, :],
                            x_d[t0 + gi0 * 128: t0 + (gi0 + 1) * 128, :]
                        ).then_inc(s_xe[gi0 % 2].h, 16)
                        s_xe[gi0 % 2].n += 16

                for qt in range(QTN):
                    @block.tensor
                    def _(t, qt=qt, sq=sq):
                        g_wf.wait(t)
                        if qt == 0:
                            ve.wait(t)
                        if qt >= 2:
                            ve.wait(t, p3_sv[(sq, qt - 2)])
                        for dc in range(EC):
                            for half in range(EH):
                                mm = t.matmul(
                                    sbig[:, (qt % 2) * E + half * EW:
                                         (qt % 2) * E + (half + 1) * EW],
                                    o_fm[:, dc * T + qt * 128:
                                         dc * T + qt * 128 + 128],
                                    wbuf[:, dc * E + half * EW:
                                         dc * E + (half + 1) * EW],
                                    start=(dc == 0), stop=(dc == EC - 1))
                        pe.inc(mm)

                    @block.vector
                    def _(v, qt=qt, sq=sq):
                        cv = Chain(v, ve)
                        bi = qt % 2
                        pe.wait(v)
                        s_zi[bi].wait(v)
                        gidx = sq * QTN + qt
                        if gidx >= 2:
                            v.wait_ge(s_out[gidx % 2].h, 16 * (gidx // 2))
                        cv.tensor_tensor(otb[bi][:, :], ztb[bi][:, :],
                                         cbc[:, :], ALU.add)

                    @block.scalar
                    def _(s, qt=qt):
                        ca = Chain(s, ae)
                        ve.wait(s)
                        ca.activation(gts[:, :], otb[qt % 2][:, :], AF.Silu)

                    @block.vector
                    def _(v, qt=qt, sq=sq):
                        cv = Chain(v, ve)
                        bi = qt % 2
                        ae.wait(v)
                        s_xe[bi].wait(v)
                        cv.tensor_tensor(otb[bi][:, :],
                                         sbig[:, bi * E:(bi + 1) * E],
                                         gts[:, :], ALU.mult)
                        cv.tensor_tensor(otb[bi][:, :], otb[bi][:, :],
                                         xtb[bi][:, :], ALU.add)
                        p3_sv[(sq, qt)] = ve.n

                    @block.sync
                    def _(sy, qt=qt, sq=sq, t0=t0):
                        bi = qt % 2
                        sy.wait_ge(ve.h, p3_sv[(sq, qt)])
                        sy.dma_start(
                            out_d[t0 + qt * 128: t0 + (qt + 1) * 128, :],
                            otb[bi][:, :]).then_inc(s_out[(sq * QTN + qt) % 2].h, 16)
                        s_out[(sq * QTN + qt) % 2].n += 16
                        gi2 = qt + 2
                        if gi2 < QTN:
                            sy.dma_start(
                                ztb[gi2 % 2][:, :],
                                z_d[t0 + gi2 * 128: t0 + (gi2 + 1) * 128, :]
                            ).then_inc(s_zi[gi2 % 2].h, 16)
                            s_zi[gi2 % 2].n += 16
                            sy.dma_start(
                                xtb[gi2 % 2][:, :],
                                x_d[t0 + gi2 * 128: t0 + (gi2 + 1) * 128, :]
                            ).then_inc(s_xe[gi2 % 2].h, 16)
                            s_xe[gi2 % 2].n += 16

                prev_sv_last[0] = p3_sv[(sq, QTN - 1)]
            p3_pe[(sq, 'end')] = pe.n
            pBes.close()

            if debug and sq == NSEQ - 1:
                @block.gpsimd
                def _(g):
                    ae.wait(g)
                    ve.wait(g)
                    for (dd, ss) in ((dq_d, q_fm), (dk_d, k_fm), (dv_d, v_tm),
                                     (do_d, o_fm)):
                        g.dma_start(dd[:, :], ss[:, :]).then_inc(g_pre.h, 16)
                        g_pre.n += 16

        @block.sync
        def _(sy):
            sy.wait_ge(s_out[0].h, s_out[0].n)
            sy.wait_ge(s_out[1].h, s_out[1].n)
            if debug:
                sy.wait_ge(s_pre.h, s_pre.n)
                sy.wait_ge(g_pre.h, g_pre.n)

    return nc


_NC_CACHE = {}


def _get_nc(key, cfg, debug=False):
    if key not in _NC_CACHE:
        _NC_CACHE[key] = build(cfg, debug=debug)
    return _NC_CACHE[key]


def make_in_maps(inputs, cfg, ncore=8):
    E, T, NSEQ, HID = cfg["E"], cfg["T"], cfg["NSEQ"], cfg["HID"]
    x = np.asarray(inputs["L0_attn_x"], np.float32)
    B = x.shape[0]
    per = B // ncore
    assert per * ncore == B and per == NSEQ

    def t(a):
        return np.ascontiguousarray(np.asarray(a, np.float32).T)

    com = {
        "wqT": t(inputs["L0_attn_Wq"]), "wkT": t(inputs["L0_attn_Wk"]),
        "wvT": t(inputs["L0_attn_Wv"]), "wpT": t(inputs["L0_attn_Wp"]),
        "bqT": t(inputs["L0_attn_Bq"]), "bkT": t(inputs["L0_attn_Bk"]),
        "bvT": t(inputs["L0_attn_Bv"]),
        "aqT": t(inputs["L0_attn_Aq"]), "akT": t(inputs["L0_attn_Ak"]),
        "avT": t(inputs["L0_attn_Av"]),
        "w1": np.ascontiguousarray(
            np.asarray(inputs["L0_ffn_W1"], np.float32).reshape(-1, HID)),
        "b1": np.asarray(inputs["L0_ffn_b1"], np.float32),
        "w2": np.ascontiguousarray(
            np.asarray(inputs["L0_ffn_W2"], np.float32).reshape(-1, E)),
        "b2": np.asarray(inputs["L0_ffn_b2"], np.float32),
        "wr": np.asarray(inputs["L0_ffn_Wr"], np.float32),
        "br": np.ascontiguousarray(
            np.asarray(inputs["L0_ffn_br"], np.float32).reshape(-1, 1)),
    }
    in_maps = []
    for c in range(ncore):
        m = dict(com)
        m["x"] = np.ascontiguousarray(
            x[c * per:(c + 1) * per].reshape(per * T, E))
        in_maps.append(m)
    return in_maps


def kernel(**inputs):
    cfg = CFG_FULL
    NCORE = 8
    T, E = cfg["T"], cfg["E"]
    in_maps = make_in_maps(inputs, cfg, NCORE)
    per = cfg["NSEQ"]
    nc = _get_nc("full", cfg)
    res = run_bass_kernel_spmd(nc, in_maps, core_ids=list(range(NCORE)))
    out = np.concatenate(
        [np.asarray(r["out"]).reshape(per, T, E) for r in res.results], axis=0)
    return out.astype(np.float32)
